# version 04
# baseline (speedup 1.0000x reference)
import numpy as np
from contextlib import ExitStack

import concourse.bass as bass
import concourse.mybir as mybir
from concourse.bass import IndirectOffsetOnAxis
from concourse.bass_utils import run_bass_kernel_spmd

F32 = mybir.dt.float32
BF16 = mybir.dt.bfloat16
I32 = mybir.dt.int32
AF = mybir.ActivationFunctionType
ALU = mybir.AluOpType
AX = mybir.AxisListType

D = 2048
S = 2048
NCORE = 8
SEQ_PC = 2
TOK = SEQ_PC * S
TB = 1024
NBLK_SEQ = S // TB
NF = D // 128
NH = 16
HG = 4
NE = 16
I_E = 1408
NI_E = I_E // 128
SI = 5632
NI_S = SI // 128
SCALE = float(128.0 ** -0.5)
EPS = 1e-5
NEG_BIG = -30000.0
BAND = 9
CAP = 384
NGC = CAP // 128
EHI = 524288.0
OOBF = float(1 << 30)


class Sem:
    __slots__ = ("h", "n", "name")

    def __init__(self, ctx, nc, name):
        self.h = ctx.enter_context(nc.semaphore(name))
        self.n = 0
        self.name = name


class Plan:
    def __init__(self):
        self.q = {k: [] for k in "tvsgy"}

    def add(self, eng, fn):
        self.q[eng].append(fn)

    def wait(self, eng, sem, val=None):
        v = sem.n if val is None else val
        if v <= 0:
            return
        self.q[eng].append(lambda E, h=sem.h, v=v: E.wait_ge(h, v))


class DStream:

    def __init__(self, ctx, nc, name, plan, queue, n_bufs=2):
        self.nb = n_bufs
        self.sems = [Sem(ctx, nc, f"{name}{i}") for i in range(n_bufs)]
        self.free = [None] * n_bufs
        self.idx = 0
        self.consumed_idx = -1
        self.plan = plan
        self.queue = queue
        self.pending = []

    def load(self, dst_ap, src_ap):
        p = self.idx % self.nb
        assert self.idx - self.nb < 0 or self.consumed_idx >= self.idx - self.nb, \
            f"stream over-prefetch: load {self.idx}, consumed {self.consumed_idx}, nb {self.nb}"
        s = self.sems[p]
        fr = self.free[p]

        def thunk(E, fr=fr, dst=dst_ap, src=src_ap, h=s.h):
            if fr is not None:
                E.wait_ge(fr[0].h, fr[1])
            E.dma_start(out=dst, in_=src).then_inc(h, 16)

        self.plan.add(self.queue, thunk)
        s.n += 16
        self.idx += 1
        self.pending.append((p, s, s.n))

    def take(self):
        return self.pending.pop(0)

    def consumed(self, p, sem):
        self.free[p] = (sem, sem.n)
        self.consumed_idx += 1


def build_program(tap=None):
    nc = bass.Bass("TRN2", target_bir_lowering=False)

    d_x = nc.dram_tensor("x_tok", [TOK, D], F32, kind="ExternalInput").ap()
    d_wdkv = nc.dram_tensor("wdkv_h", [128, NF * 128], F32, kind="ExternalInput").ap()
    d_wq = nc.dram_tensor("wq_T", [D, D], F32, kind="ExternalInput").ap()
    d_wuk = nc.dram_tensor("wuk_h", [128, NH * 128], F32, kind="ExternalInput").ap()
    d_wuv = nc.dram_tensor("wuv_T", [128, NH * 128], F32, kind="ExternalInput").ap()
    d_wo = nc.dram_tensor("wo_T", [D, D], F32, kind="ExternalInput").ap()
    d_gw = nc.dram_tensor("gw_h", [128, NF * NE], F32, kind="ExternalInput").ap()
    d_sgv = nc.dram_tensor("sgv_h", [128, NF], F32, kind="ExternalInput").ap()
    d_sgu = nc.dram_tensor("sgu_c", [NI_S * 2, 128, NF * 128], F32, kind="ExternalInput").ap()
    d_sd = nc.dram_tensor("sd_T", [SI, D], F32, kind="ExternalInput").ap()
    d_egu = nc.dram_tensor("egu_c", [NE, NI_E * 2, 128, NF * 128], F32, kind="ExternalInput").ap()
    d_ed = nc.dram_tensor("ed_T", [NE, I_E, D], F32, kind="ExternalInput").ap()
    d_alibi = nc.dram_tensor("alibi_c", [128, BAND * 128], F32, kind="ExternalInput").ap()
    d_ident = nc.dram_tensor("ident_c", [128, 128], F32, kind="ExternalInput").ap()
    d_ones = nc.dram_tensor("ones_c", [1, 128], F32, kind="ExternalInput").ap()
    d_L = nc.dram_tensor("Lh_c", [128, 128], F32, kind="ExternalInput").ap()
    d_U = nc.dram_tensor("Uh_c", [128, 128], F32, kind="ExternalInput").ap()
    d_iota = nc.dram_tensor("iota_c", [128, CAP], F32, kind="ExternalInput").ap()
    d_out = nc.dram_tensor("out_tok", [TOK, D], F32, kind="ExternalOutput").ap()
    d_xntm = nc.dram_tensor("xn_tm", [TB, D], BF16, kind="Internal").ap()

    with ExitStack() as ctx:
        eC = ctx.enter_context

        xnT = eC(nc.sbuf_tensor("xnT", [128, NF * TB], BF16))
        acc = eC(nc.sbuf_tensor("acc", [128, NF * TB], F32))
        ckvT = eC(nc.sbuf_tensor("ckvT", [128, S], BF16))
        ckvtm = eC(nc.sbuf_tensor("ckvtm", [128, NF * 128], BF16))
        obuf = eC(nc.sbuf_tensor("obuf", [128, HG * TB], BF16))
        qt4 = eC(nc.sbuf_tensor("qt4", [128, HG * TB], BF16))
        alibi = eC(nc.sbuf_tensor("alibi", [128, BAND * 128], BF16))
        wdkv = eC(nc.sbuf_tensor("wdkv", [128, NF * 128], BF16))
        wuk = eC(nc.sbuf_tensor("wuk", [128, NH * 128], BF16))
        wuvT = eC(nc.sbuf_tensor("wuvT", [128, NH * 128], BF16))
        gw = eC(nc.sbuf_tensor("gw", [128, NF * NE], BF16))
        sgv = eC(nc.sbuf_tensor("sgv", [128, NF], BF16))
        identb = eC(nc.sbuf_tensor("identb", [128, 128], BF16))
        onesb = eC(nc.sbuf_tensor("onesb", [1, 128], BF16))
        Lb = eC(nc.sbuf_tensor("Lb", [128, 128], BF16))
        Ub = eC(nc.sbuf_tensor("Ub", [128, 128], BF16))
        mskb_all = eC(nc.sbuf_tensor("mskb_all", [128, 128], BF16))
        comb_all = eC(nc.sbuf_tensor("comb_all", [128, 128], BF16))
        slotf = eC(nc.sbuf_tensor("slotf", [128, 128], F32))
        ubuf = eC(nc.sbuf_tensor("ubuf", [128, 3 * NF * 128], BF16))
        dbuf = eC(nc.sbuf_tensor("dbuf", [128, 3 * 512], BF16))
        wqb = eC(nc.sbuf_tensor("wqb", [128, 3 * 512], BF16))
        R1f = eC(nc.sbuf_tensor("R1f", [128, 4096], F32))
        R2b = eC(nc.sbuf_tensor("R2b", [128, NI_E * TB], BF16))
        R3b = eC(nc.sbuf_tensor("R3b", [128, 2048], BF16))
        cbc = eC(nc.sbuf_tensor("cbc", [128, CAP], BF16))
        sbc = eC(nc.sbuf_tensor("sbc", [128, TB], BF16))
        rowb = eC(nc.sbuf_tensor("rowb", [1, TB + CAP], BF16))
        cT = eC(nc.sbuf_tensor("cT", [NE, TB], BF16))
        iotaC = eC(nc.sbuf_tensor("iotaC", [128, CAP], F32))
        Rcmp = eC(nc.sbuf_tensor("Rcmp", [128, 2 * CAP], BF16))
        stats = eC(nc.sbuf_tensor("stats", [128, 12], F32))
        sumr, ssq, negm, vep, lnv, rstd, nmax, sums, rsum, s4, l4, r4 = (
            stats[:, i:i + 1] for i in range(12))
        t8 = eC(nc.sbuf_tensor("t8", [128, 8], F32))
        Esb = eC(nc.sbuf_tensor("Esb", [128, NE], F32))
        msk = eC(nc.sbuf_tensor("msk", [128, NE], F32))
        cw = eC(nc.sbuf_tensor("cw", [128, NE], F32))
        comb = eC(nc.sbuf_tensor("comb", [128, NE], BF16))

        srw = rowb[:, 0:TB]
        crow = rowb[:, TB:TB + CAP]

        psA = eC(nc.psum_tensor("psA", [128, 1024], F32))
        psB = eC(nc.psum_tensor("psB", [128, 1024], F32))
        psC = eC(nc.psum_tensor("psC", [128, 1024], F32))
        psD = eC(nc.psum_tensor("psD", [128, 1024], F32))
        PS = [psA, psB, psC, psD]

        sv = Sem(ctx, nc, "sv")
        sa = Sem(ctx, nc, "sa")
        st = Sem(ctx, nc, "st")
        ssm = Sem(ctx, nc, "ssm")
        sout = Sem(ctx, nc, "sout")
        s_idx = Sem(ctx, nc, "sidx")
        s_xn = Sem(ctx, nc, "sxn")
        s_acm = Sem(ctx, nc, "sacm")
        s_gx = Sem(ctx, nc, "sgx")
        s_sc = Sem(ctx, nc, "ssc")
        s_cv = Sem(ctx, nc, "scv")
        s_fi = Sem(ctx, nc, "sfi")

        plan = Plan()
        W = plan.wait
        st_wu = DStream(ctx, nc, "wu", plan, "g", n_bufs=3)
        st_wd = DStream(ctx, nc, "wd", plan, "g", n_bufs=3)
        st_wq = DStream(ctx, nc, "wq", plan, "g", n_bufs=3)
        st_x = DStream(ctx, nc, "sx", plan, "y")
        st_g = DStream(ctx, nc, "sg", plan, "y")

        def mk_pump(stream, srcs, dst_fn):
            base = stream.idx
            state = {"next": 0}

            def pump():
                while (state["next"] < len(srcs)
                       and base + state["next"] <= stream.consumed_idx + stream.nb - 1):
                    j = state["next"]
                    stream.load(dst_fn(base + j), srcs[j])
                    state["next"] += 1
            return pump

        ps_free = {}
        silu_free = [None, None]

        def ps_wait(ps):
            fr = ps_free.get(ps.name, None)
            if fr is not None:
                W("t", fr[0], fr[1])

        def ps_release(ps, sem):
            ps_free[ps.name] = (sem, sem.n)

        def ubuf_slot(j):
            return ubuf[:, (j % 3) * 2048:((j % 3) + 1) * 2048]

        def dbuf_slot(j):
            return dbuf[:, (j % 3) * 512:((j % 3) + 1) * 512]

        def wqb_slot(j):
            return wqb[:, (j % 3) * 512:((j % 3) + 1) * 512]

        P_OFF, PT_OFF, P2_OFF = 0, 1152, 2304
        SPv = R2b[:, 3456:5760].bitcast(F32)
        QH_OFF = 5760
        HP_OFF, XG_OFF = 0, NI_E * CAP
        R2bF = R2b[:, :].bitcast(F32)
        qt4F = qt4[:, :].bitcast(F32)
        obufF = obuf[:, :].bitcast(F32)

        def T(fn, inc=None, k=1):
            if inc is None:
                plan.add("t", fn)
            else:
                plan.add("t", lambda E, f=fn, h=inc.h, kk=k: f(E).then_inc(h, kk))
                inc.n += k

        def V(fn, inc=None, k=1):
            if inc is None:
                plan.add("v", fn)
            else:
                plan.add("v", lambda E, f=fn, h=inc.h, kk=k: f(E).then_inc(h, kk))
                inc.n += k

        def A(fn, inc=None, k=1):
            if inc is None:
                plan.add("s", fn)
            else:
                plan.add("s", lambda E, f=fn, h=inc.h, kk=k: f(E).then_inc(h, kk))
                inc.n += k

        def G(fn, inc=None, k=16):
            if inc is None:
                plan.add("g", fn)
            else:
                plan.add("g", lambda E, f=fn, h=inc.h, kk=k: f(E).then_inc(h, kk))
                inc.n += k

        def Y(fn, inc=None, k=16):
            if inc is None:
                plan.add("y", fn)
            else:
                plan.add("y", lambda E, f=fn, h=inc.h, kk=k: f(E).then_inc(h, kk))
                inc.n += k

        def phase_gate():
            W("t", sv); W("t", sa)
            W("v", st); W("v", sa)
            W("s", st); W("s", sv)

        def vrt():
            V(lambda E: E.engine_nop(), inc=sv)
            W("s", sv)
            A(lambda E: E.nop(), inc=sa)
            W("v", sa)

        def tap_dump(name, ap):
            if tap is None:
                return
            dr = nc.dram_tensor("tap_" + name, list(ap.shape), ap.dtype,
                                kind="ExternalOutput").ap()
            W("y", sv); W("y", sa); W("y", st)
            Y(lambda E, d=dr, s_=ap: E.dma_start(out=d, in_=s_), inc=sout)

        def xn(g, t0, w):
            return xnT[:, g * TB + t0: g * TB + t0 + w]

        def accv(g, t0, w):
            return acc[:, g * TB + t0: g * TB + t0 + w]

        for dst, src in ((wdkv, d_wdkv), (wuk, d_wuk), (wuvT, d_wuv), (gw, d_gw),
                         (sgv, d_sgv), (identb, d_ident), (onesb, d_ones),
                         (alibi, d_alibi), (Lb, d_L), (Ub, d_U), (iotaC, d_iota)):
            plan.add("g", lambda E, d=dst, s_=src, h=ssm.h:
                     E.dma_start(out=d[:], in_=s_).then_inc(h, 16))
            ssm.n += 16
        for e_ in "tvs":
            W(e_, ssm)

        st_x.load(R1f[:, 0:2048], d_x[0:128, :])

        def p0(s_i, b_i):
            kb = s_i * NBLK_SEQ + b_i
            phase_gate()
            for tt in range(8):
                gt = kb * 8 + tt
                par, xsem, xval = st_x.take()
                xa = R1f[:, par * 2048:(par + 1) * 2048]
                if gt + 1 < TOK // 128:
                    st_x.load(R1f[:, ((gt + 1) % 2) * 2048:(((gt + 1) % 2) + 1) * 2048],
                              d_x[(gt + 1) * 128:(gt + 2) * 128, :])
                W("v", xsem, xval)
                V(lambda E, x_=xa: E.reduce_sum(sumr, x_, axis=AX.X), inc=sv)
                W("s", sv)
                A(lambda E: E.nop(), inc=sa)
                W("v", sa)
                V(lambda E: E.tensor_scalar(negm, sumr, -1.0 / D, None, op0=ALU.mult),
                  inc=sv)
                W("s", sv)
                A(lambda E: E.nop(), inc=sa)
                W("v", sa)
                V(lambda E, x_=xa: E.tensor_scalar(R3b[:, 0:2048], x_, negm, None,
                                                   op0=ALU.add), inc=sv)
                st_x.consumed(par, sv)
                W("s", sv)
                A(lambda E: E.activation(R2b[:, 0:2048], R3b[:, 0:2048], AF.Square,
                                         accum_out=ssq), inc=sa)
                W("v", sa)
                V(lambda E: E.tensor_scalar(vep, ssq, 1.0 / D, EPS,
                                            op0=ALU.mult, op1=ALU.add), inc=sv)
                W("s", sv)
                A(lambda E: E.activation(lnv, vep, AF.Ln), inc=sa)
                W("v", sa)
                V(lambda E: E.engine_nop(), inc=sv)
                W("s", sv)
                A(lambda E: E.activation(rstd, lnv, AF.Exp, scale=-0.5), inc=sa)
                W("v", sa)
                W("v", s_xn)
                V(lambda E: E.tensor_scalar(R2b[:, 2048:4096], R3b[:, 0:2048],
                                            rstd, None, op0=ALU.mult), inc=sv)
                W("y", sv)
                Y(lambda E, tt=tt: E.dma_start(out=d_xntm[tt * 128:(tt + 1) * 128, :],
                                               in_=R2b[:, 2048:4096]), inc=s_xn)
                W("t", sv)
                pdb = psD[:, :].bitcast(BF16)
                for g in range(NF):
                    T(lambda E, g=g, pdb=pdb: E.transpose(
                        pdb[:, g * 128:(g + 1) * 128],
                        R2b[:, 2048 + g * 128: 2048 + (g + 1) * 128], identb[:]),
                      inc=(st if g == NF - 1 else None))
                W("v", st)
                dst3 = xnT[:, :].rearrange("p (g t) -> p g t", t=TB)[:, :, tt * 128:(tt + 1) * 128]
                src3 = pdb.rearrange("p (g t) -> p g t", t=128)
                V(lambda E, d3=dst3, s3=src3: E.tensor_copy(d3, s3), inc=sv)

            W("t", sv)
            boff = b_i * TB
            ps_wait(psA)
            for th in range(2):
                for g in range(NF):
                    T(lambda E, th=th, g=g: E.matmul(
                        psA[:, th * 512:(th + 1) * 512],
                        wdkv[:, g * 128:(g + 1) * 128],
                        xn(g, th * 512, 512),
                        start=(g == 0), stop=(g == NF - 1)),
                      inc=(st if (th == 1 and g == NF - 1) else None))
            W("s", st)
            A(lambda E, bo=boff: E.copy(ckvT[:, bo:bo + TB], psA[:, :]), inc=sa)
            ps_release(psA, sa)
            W("t", sa)
            pdb = psD[:, :].bitcast(BF16)
            for jc in range(8):
                T(lambda E, jc=jc, bo=boff, pdb=pdb: E.transpose(
                    pdb[:, jc * 128:(jc + 1) * 128],
                    ckvT[:, bo + jc * 128: bo + (jc + 1) * 128], identb[:]),
                  inc=(st if jc == 7 else None))
            W("v", st)
            dst3 = ckvtm[:, (b_i * 8) * 128:(b_i * 8 + 8) * 128]
            V(lambda E, d_=dst3, pdb=pdb: E.tensor_copy(d_, pdb[:, 0:1024]), inc=sv)

            psI = psB[:, 512:528]
            for tt in range(8):
                W("t", sv)
                for g in range(NF):
                    T(lambda E, tt=tt, g=g: E.matmul(
                        psB[:, 0:NE],
                        xn(g, tt * 128, 128),
                        gw[:, g * NE:(g + 1) * NE],
                        start=(g == 0), stop=(g == NF - 1)),
                      inc=(st if g == NF - 1 else None))
                W("v", st)
                V(lambda E: E.reduce_max(nmax, psB[:, 0:NE], axis=AX.X, negate=True), inc=sv)
                W("s", sv)
                A(lambda E: E.activation(Esb[:], psB[:, 0:NE], AF.Exp, bias=nmax), inc=sa)
                W("v", sa)
                V(lambda E: E.max(out=t8[:], in_=Esb[:]), inc=sv)
                W("s", sv)
                A(lambda E: E.nop(), inc=sa)
                W("v", sa)
                V(lambda E: E.tensor_scalar(msk[:], Esb[:], t8[:, 3:4], None, op0=ALU.is_ge),
                  inc=sv)
                W("s", sv)
                A(lambda E: E.nop(), inc=sa)
                W("v", sa)
                V(lambda E: E.tensor_mul(cw[:], Esb[:], msk[:]))
                V(lambda E, tt=tt: E.tensor_copy(mskb_all[:, tt * NE:(tt + 1) * NE], msk[:]))
                V(lambda E: E.tensor_scalar(msk[:], msk[:], -4096.0, 4096.0,
                                            op0=ALU.mult, op1=ALU.add), inc=sv)
                W("s", sv)
                A(lambda E: E.nop(), inc=sa)
                W("v", sa)
                V(lambda E: E.reduce_sum(s4, cw[:], axis=AX.X), inc=sv)
                W("s", sv)
                A(lambda E: E.activation(l4, s4, AF.Ln), inc=sa)
                W("v", sa)
                V(lambda E: E.engine_nop(), inc=sv)
                W("s", sv)
                A(lambda E: E.activation(r4, l4, AF.Exp, scale=-1.0), inc=sa)
                W("v", sa)
                V(lambda E: E.tensor_scalar(comb[:], cw[:], r4, None, op0=ALU.mult), inc=sv)
                W("t", sv)
                T(lambda E, tt=tt: E.matmul(psI, Lb[:], mskb_all[:, tt * NE:(tt + 1) * NE],
                                            start=(tt == 0), stop=False), inc=st)
                W("v", st)
                V(lambda E, tt=tt: E.scalar_tensor_tensor(
                    slotf[:, tt * NE:(tt + 1) * NE], psI, 1.0, msk[:],
                    op0=ALU.mult, op1=ALU.add))
                V(lambda E, tt=tt: E.tensor_copy(comb_all[:, tt * NE:(tt + 1) * NE],
                                                 comb[:]), inc=sv)
                W("t", sv)
                T(lambda E, tt=tt: E.matmul(psI, Ub[:], mskb_all[:, tt * NE:(tt + 1) * NE],
                                            start=False, stop=(tt == 7)), inc=st)
                pdb16 = psD[:, :].bitcast(BF16)
                T(lambda E, tt=tt, pdb16=pdb16: E.transpose(
                    pdb16[0:NE, tt * 128:(tt + 1) * 128], comb[:], identb[:]), inc=st)
                W("v", st)
                V(lambda E, tt=tt, pdb16=pdb16: E.tensor_copy(
                    cT[:, tt * 128:(tt + 1) * 128],
                    pdb16[0:NE, tt * 128:(tt + 1) * 128]), inc=sv)

            W("t", sv)
            for th in range(2):
                for g in range(NF):
                    T(lambda E, th=th, g=g: E.matmul(
                        psB[0:1, th * 512:(th + 1) * 512],
                        sgv[:, g:g + 1],
                        xn(g, th * 512, 512),
                        start=(g == 0), stop=(g == NF - 1)),
                      inc=(st if (th == 1 and g == NF - 1) else None))
            W("s", st)
            A(lambda E: E.activation(srw, psB[0:1, :], AF.Sigmoid), inc=sa)
            ps_wait(psC)
            W("t", sa)
            for th in range(2):
                T(lambda E, th=th: E.matmul(psC[:, th * 512:(th + 1) * 512], onesb[:],
                                            srw[:, th * 512:(th + 1) * 512],
                                            start=True, stop=True),
                  inc=(st if th == 1 else None))
            W("v", st)
            V(lambda E: E.tensor_copy(sbc[:], psC[:, :]), inc=sv)
            ps_release(psC, sv)
            ps_release(psB, sv)
            ps_release(psD, sv)

        def attn(s_i, b_i, hg):
            phase_gate()
            if hg == 0:
                W("s", s_acm)
                W("v", s_acm)
            pump_q = mk_pump(st_wq,
                             [d_wq[f * 128:(f + 1) * 128, hg * 512:(hg + 1) * 512]
                              for f in range(NF)],
                             wqb_slot)
            pump_q()
            for f in range(NF):
                par, qsem, qval = st_wq.take()
                W("t", qsem, qval)
                if f == 0:
                    for p_ in PS:
                        ps_wait(p_)
                for hh in range(HG):
                    for th in range(2):
                        T(lambda E, f=f, hh=hh, th=th, par=par: E.matmul(
                            PS[hh][:, th * 512:(th + 1) * 512],
                            wqb[:, par * 512 + hh * 128: par * 512 + (hh + 1) * 128],
                            xn(f, th * 512, 512),
                            start=(f == 0), stop=(f == NF - 1)),
                          inc=(st if (hh == HG - 1 and th == 1) else None))
                st_wq.consumed(par, st)
                pump_q()
            W("s", st)
            for hh in range(HG):
                A(lambda E, hh=hh: E.copy(R2b[:, QH_OFF + hh * TB: QH_OFF + (hh + 1) * TB],
                                          PS[hh][:, :]), inc=(sa if hh == HG - 1 else None))
            W("t", sa)
            for hh in range(HG):
                h = hg * HG + hh
                for th in range(2):
                    T(lambda E, hh=hh, h=h, th=th: E.matmul(
                        PS[hh][:, th * 512:(th + 1) * 512],
                        wuk[:, h * 128:(h + 1) * 128],
                        R2b[:, QH_OFF + hh * TB + th * 512: QH_OFF + hh * TB + (th + 1) * 512],
                        start=True, stop=True),
                      inc=(st if (hh == HG - 1 and th == 1) else None))
            W("s", st)
            for hh in range(HG):
                A(lambda E, hh=hh: E.activation(qt4[:, hh * TB:(hh + 1) * TB], PS[hh][:, :],
                                                AF.Identity, scale=SCALE),
                  inc=(sa if hh == HG - 1 else None))
            W("t", sa); W("v", sa)

            pBb = psB[:, :].bitcast(BF16)
            for qt in range(8):
                Qg = b_i * 8 + qt
                js = max(0, Qg - (BAND - 1))
                j0b = js * 128
                wband = (Qg + 1) * 128 - j0b
                for hh in range(HG):
                    h = hg * HG + hh
                    nchunks = (wband + 511) // 512
                    for jc5 in range(nchunks):
                        jo = jc5 * 512
                        w_ = min(512, wband - jo)
                        T(lambda E, hh=hh, qt=qt, jo=jo, w_=w_, p_=(jc5 % 2), j0b=j0b: E.matmul(
                            psA[:, p_ * 512: p_ * 512 + w_],
                            qt4[:, hh * TB + qt * 128: hh * TB + (qt + 1) * 128],
                            ckvT[:, j0b + jo: j0b + jo + w_],
                            start=True, stop=True), inc=st)
                        W("v", st)
                        V(lambda E, jo=jo, w_=w_, p_=(jc5 % 2), wb=wband: E.tensor_tensor(
                            SPv[:, jo: jo + w_],
                            psA[:, p_ * 512: p_ * 512 + w_],
                            alibi[:, BAND * 128 - wb + jo: BAND * 128 - wb + jo + w_],
                            op=ALU.add), inc=sv)
                        W("t", sv)
                    V(lambda E, wb=wband: E.reduce_max(nmax, SPv[:, 0:wb],
                                                       axis=AX.X, negate=True), inc=sv)
                    W("s", sv)
                    A(lambda E, wb=wband: E.activation(R2b[:, P_OFF:P_OFF + wb],
                                                       SPv[:, 0:wb], AF.Exp,
                                                       bias=nmax, accum_out=sums), inc=sa)
                    W("v", sa)
                    V(lambda E: E.reciprocal(rsum, sums), inc=sv)
                    W("s", sv)
                    A(lambda E: E.nop(), inc=sa)
                    W("v", sa)
                    V(lambda E, wb=wband: E.tensor_scalar(R2b[:, P2_OFF:P2_OFF + wb],
                                                          R2b[:, P_OFF:P_OFF + wb],
                                                          rsum, None, op0=ALU.mult),
                      inc=sv)
                    W("t", sv)
                    nj = wband // 128
                    for jc in range(nj):
                        T(lambda E, jc=jc: E.transpose(
                            pBb[:, jc * 128:(jc + 1) * 128],
                            R2b[:, P2_OFF + jc * 128: P2_OFF + (jc + 1) * 128],
                            identb[:]),
                          inc=(st if jc == nj - 1 else None))
                    W("s", st)
                    A(lambda E, wb=wband: E.copy(R2b[:, PT_OFF:PT_OFF + wb], pBb[:, 0:wb]),
                      inc=sa)
                    W("t", sa)
                    for jc in range(nj):
                        T(lambda E, jc=jc, js=js, nj=nj: E.matmul(
                            psC[:, 0:128],
                            ckvtm[:, (js + jc) * 128:(js + jc + 1) * 128],
                            R2b[:, PT_OFF + jc * 128: PT_OFF + (jc + 1) * 128],
                            start=(jc == 0), stop=(jc == nj - 1)),
                          inc=(st if jc == nj - 1 else None))
                    W("v", st)
                    V(lambda E: E.tensor_copy(R3b[:, 0:128], psC[:, 0:128]), inc=sv)
                    W("t", sv)
                    T(lambda E, h=h: E.matmul(psC[:, 512:640],
                                              wuvT[:, h * 128:(h + 1) * 128],
                                              R3b[:, 0:128], start=True, stop=True), inc=st)
                    W("s", st)
                    A(lambda E, hh=hh, qt=qt: E.copy(
                        obuf[:, hh * TB + qt * 128: hh * TB + (qt + 1) * 128],
                        psC[:, 512:640]), inc=sa)
            W("v", sa)
            V(lambda E: E.engine_nop(), inc=sv)
            for p_ in PS:
                ps_release(p_, sv)

        def woadd(s_i, b_i, hg, init):
            phase_gate()
            pump_wo = mk_pump(st_wd,
                              [d_wo[(hg * HG + hh) * 128:(hg * HG + hh + 1) * 128,
                                    gq * 512:(gq + 1) * 512]
                               for gq in range(4) for hh in range(HG)],
                              dbuf_slot)
            pump_wo()
            for gq in range(4):
                for hh in range(HG):
                    par, dsem, dval = st_wd.take()
                    W("t", dsem, dval)
                    for gl in range(4):
                        if hh == 0:
                            ps_wait(PS[gl])
                        for th in range(2):
                            T(lambda E, par=par, gl=gl, th=th, hh=hh: E.matmul(
                                PS[gl][:, th * 512:(th + 1) * 512],
                                dbuf[:, par * 512 + gl * 128: par * 512 + (gl + 1) * 128],
                                obuf[:, hh * TB + th * 512: hh * TB + (th + 1) * 512],
                                start=(hh == 0), stop=(hh == HG - 1)),
                              inc=(st if (gl == 3 and th == 1) else None))
                    st_wd.consumed(par, st)
                    pump_wo()
                W("v", st)
                for gl in range(4):
                    g = gq * 4 + gl
                    if init:
                        V(lambda E, g=g, gl=gl: E.tensor_copy(accv(g, 0, TB), PS[gl][:, :]),
                          inc=sv)
                    else:
                        V(lambda E, g=g, gl=gl: E.tensor_tensor(accv(g, 0, TB), PS[gl][:, :],
                                                                accv(g, 0, TB), op=ALU.add),
                          inc=sv)
                    ps_release(PS[gl], sv)

        def ffn_up_pairs_dense(npairs, bcast_sb, slab_iter):
            srcs = [ap for pair in slab_iter for ap in pair]
            pump_u = mk_pump(st_wu, srcs, ubuf_slot)
            pump_u()
            for k in range(npairs):
                psg = PS[(2 * k) % 4]
                psu = PS[(2 * k + 1) % 4]
                sgo = (k % 2) * TB
                par, usem, uval = st_wu.take()
                W("t", usem, uval)
                ps_wait(psg)
                for g in range(NF):
                    for th in range(2):
                        T(lambda E, par=par, g=g, th=th, ps_=psg: E.matmul(
                            ps_[:, th * 512:(th + 1) * 512],
                            ubuf[:, par * 2048 + g * 128: par * 2048 + (g + 1) * 128],
                            xn(g, th * 512, 512),
                            start=(g == 0), stop=(g == NF - 1)),
                          inc=(st if (g == NF - 1 and th == 1) else None))
                st_wu.consumed(par, st)
                pump_u()
                W("s", st)
                if silu_free[k % 2] is not None:
                    W("s", silu_free[k % 2][0], silu_free[k % 2][1])
                A(lambda E, ps_=psg, sgo=sgo: E.activation(R3b[:, sgo:sgo + TB],
                                                           ps_[:, :], AF.Silu), inc=sa)
                ps_release(psg, sa)
                par, usem, uval = st_wu.take()
                W("t", usem, uval)
                ps_wait(psu)
                for g in range(NF):
                    for th in range(2):
                        T(lambda E, par=par, g=g, th=th, ps_=psu: E.matmul(
                            ps_[:, th * 512:(th + 1) * 512],
                            ubuf[:, par * 2048 + g * 128: par * 2048 + (g + 1) * 128],
                            xn(g, th * 512, 512),
                            start=(g == 0), stop=(g == NF - 1)),
                          inc=(st if (g == NF - 1 and th == 1) else None))
                st_wu.consumed(par, st)
                pump_u()
                W("v", st); W("v", sa)
                V(lambda E, ps_=psu, bc=bcast_sb, k=k: E.tensor_tensor(
                    R2b[:, k * TB:(k + 1) * TB], ps_[:, :], bc[:], op=ALU.mult))
                V(lambda E, k=k, sgo=sgo: E.tensor_tensor(
                    R2b[:, k * TB:(k + 1) * TB], R2b[:, k * TB:(k + 1) * TB],
                    R3b[:, sgo:sgo + TB], op=ALU.mult), inc=sv)
                ps_release(psu, sv)
                silu_free[k % 2] = (sv, sv.n)

        def ffn_down_dense(nchunks, slab_fn, init):
            pump_d = mk_pump(st_wd,
                             [slab_fn(i, gq) for gq in range(4) for i in range(nchunks)],
                             dbuf_slot)
            pump_d()
            for gq in range(4):
                for i in range(nchunks):
                    par, dsem, dval = st_wd.take()
                    W("t", dsem, dval)
                    for gl in range(4):
                        if i == 0:
                            ps_wait(PS[gl])
                        for th in range(2):
                            T(lambda E, par=par, gl=gl, th=th, i=i: E.matmul(
                                PS[gl][:, th * 512:(th + 1) * 512],
                                dbuf[:, par * 512 + gl * 128: par * 512 + (gl + 1) * 128],
                                R2b[:, i * TB + th * 512: i * TB + (th + 1) * 512],
                                start=(i == 0), stop=(i == nchunks - 1)),
                              inc=(st if (gl == 3 and th == 1) else None))
                    st_wd.consumed(par, st)
                    pump_d()
                W("v", st)
                for gl in range(4):
                    g = gq * 4 + gl
                    if init:
                        V(lambda E, g=g, gl=gl: E.tensor_copy(accv(g, 0, TB), PS[gl][:, :]),
                          inc=sv)
                    else:
                        V(lambda E, g=g, gl=gl: E.tensor_tensor(accv(g, 0, TB), PS[gl][:, :],
                                                                accv(g, 0, TB), op=ALU.add),
                          inc=sv)
                    ps_release(PS[gl], sv)

        def shared(s_i, b_i):
            phase_gate()
            for gr in range(4):
                def slabs():
                    for k in range(11):
                        yield (d_sgu[gr * 11 + k, :, :], d_sgu[NI_S + gr * 11 + k, :, :])
                ffn_up_pairs_dense(11, sbc, slabs())
                ffn_down_dense(11, lambda i, gq, gr=gr:
                               d_sd[(gr * 11 + i) * 128:(gr * 11 + i + 1) * 128,
                                    gq * 512:(gq + 1) * 512],
                               init=(gr == 0))

        def expert_gather(e, b_i):
            pDb = psD[:, :].bitcast(BF16)
            pump_g = mk_pump(st_g, [d_xntm[(t % 8) * 128:((t % 8) + 1) * 128, :] for t in range(16)],
                             lambda j: obuf[:, (j % 2) * 2048:((j % 2) + 1) * 2048])
            pump_g()
            for gen, scs in ((0, (0, 1)), (1, (2,))):
                for tt in range(8):
                    par, gsem, gval = st_g.take()
                    rco = (tt % 2) * CAP
                    W("v", st)
                    V(lambda E, tt=tt, e=e, rco=rco: E.tensor_scalar(
                        Rcmp[:, rco:rco + CAP], iotaC[:],
                        slotf[:, tt * NE + e: tt * NE + e + 1], None,
                        op0=ALU.is_equal), inc=sv)
                    W("t", gsem, gval)
                    W("t", sv)
                    if tt == 0:
                        for p_ in PS:
                            ps_wait(p_)
                    for k, sc in enumerate(scs):
                        psx = [(psA, psB), (psC, psD)][k]
                        for fh in range(4):
                            T(lambda E, tt=tt, sc=sc, fh=fh, par=par, rco=rco,
                              ps_=psx[fh // 2]: E.matmul(
                                ps_[:, (fh % 2) * 512:(fh % 2 + 1) * 512],
                                Rcmp[:, rco + sc * 128: rco + (sc + 1) * 128],
                                obuf[:, par * 2048 + fh * 512: par * 2048 + (fh + 1) * 512],
                                start=(tt == 0), stop=(tt == 7)),
                              inc=(st if (k == len(scs) - 1 and fh == 3) else None))
                    if gen == 1:
                        if tt == 0:
                            ps_wait(psC)
                        T(lambda E, tt=tt, e=e, rco=rco: E.matmul(
                            psC[0:1, 0:CAP],
                            comb_all[:, tt * NE + e: tt * NE + e + 1],
                            Rcmp[:, rco:rco + CAP],
                            start=(tt == 0), stop=(tt == 7)),
                          inc=(st if tt == 7 else None))
                    st_g.consumed(par, st)
                    pump_g()
                qb = qt4[:, :].bitcast(BF16)
                pdb = psD[:, :].bitcast(BF16)
                stage = []
                W("s", st)
                for k, sc in enumerate(scs):
                    psx = [(psA, psB), (psC, psD)][k]
                    dst = R3b[:, 0:2048] if k == 0 else qb[:, 0:2048]
                    A(lambda E, ps_=psx[0], d_=dst[:, 0:1024]: E.copy(d_, ps_[:, :]))
                    A(lambda E, ps_=psx[1], d_=dst[:, 1024:2048]: E.copy(d_, ps_[:, :]),
                      inc=sa)
                    ps_release(psx[0], sa)
                    ps_release(psx[1], sa)
                    stage.append((sc, dst))
                for sc, stg in stage:
                    W("t", sa)
                    ps_wait(psD)
                    for c in range(NF):
                        T(lambda E, c=c, pdb=pdb, stg=stg: E.transpose(
                            pdb[:, c * 128:(c + 1) * 128],
                            stg[:, c * 128:(c + 1) * 128], identb[:]),
                          inc=(st if c == NF - 1 else None))
                    W("s", st)
                    dst3 = R2b[:, XG_OFF:XG_OFF + NF * CAP].rearrange(
                        "p (c s) -> p c s", s=CAP)[:, :, sc * 128:(sc + 1) * 128]
                    src3 = pdb.rearrange("p (c s) -> p c s", s=128)
                    A(lambda E, d3=dst3, s3=src3: E.copy(d3, s3), inc=sa)
                    ps_release(psD, sa)
            W("v", st)
            V(lambda E: E.tensor_copy(crow, psC[0:1, 0:CAP]), inc=sv)
            ps_release(psC, sv)
            W("t", sv)
            ps_wait(psD)
            T(lambda E: E.matmul(psD[:, 0:CAP], onesb[:], crow, start=True, stop=True),
              inc=st)
            W("v", st)
            V(lambda E: E.tensor_copy(cbc[:, 0:CAP], psD[:, 0:CAP]), inc=sv)
            ps_release(psD, sv)

        def expert_up(e):
            srcs = []
            for k in range(11):
                srcs += [d_egu[e, k, :, :], d_egu[e, 11 + k, :, :]]
            pump_u = mk_pump(st_wu, srcs, ubuf_slot)
            pump_u()
            for k in range(11):
                psg = [psA, psC][k % 2]
                psu = [psB, psD][k % 2]
                sgo = (k % 2) * TB
                par, usem, uval = st_wu.take()
                W("t", usem, uval)
                ps_wait(psg)
                for c in range(NF):
                    T(lambda E, par=par, c=c, ps_=psg: E.matmul(
                        ps_[:, 0:CAP],
                        ubuf[:, par * 2048 + c * 128: par * 2048 + (c + 1) * 128],
                        R2b[:, XG_OFF + c * CAP: XG_OFF + (c + 1) * CAP],
                        start=(c == 0), stop=(c == NF - 1)),
                      inc=(st if c == NF - 1 else None))
                st_wu.consumed(par, st)
                pump_u()
                W("s", st)
                if silu_free[k % 2] is not None:
                    W("s", silu_free[k % 2][0], silu_free[k % 2][1])
                A(lambda E, ps_=psg, sgo=sgo: E.activation(R3b[:, sgo:sgo + CAP],
                                                           ps_[:, 0:CAP], AF.Silu), inc=sa)
                ps_release(psg, sa)
                par, usem, uval = st_wu.take()
                W("t", usem, uval)
                ps_wait(psu)
                for c in range(NF):
                    T(lambda E, par=par, c=c, ps_=psu: E.matmul(
                        ps_[:, 0:CAP],
                        ubuf[:, par * 2048 + c * 128: par * 2048 + (c + 1) * 128],
                        R2b[:, XG_OFF + c * CAP: XG_OFF + (c + 1) * CAP],
                        start=(c == 0), stop=(c == NF - 1)),
                      inc=(st if c == NF - 1 else None))
                st_wu.consumed(par, st)
                pump_u()
                W("v", st); W("v", sa)
                V(lambda E, ps_=psu, k=k: E.tensor_tensor(
                    R2b[:, HP_OFF + k * CAP: HP_OFF + (k + 1) * CAP],
                    ps_[:, 0:CAP], cbc[:, 0:CAP], op=ALU.mult))
                V(lambda E, k=k, sgo=sgo: E.tensor_tensor(
                    R2b[:, HP_OFF + k * CAP: HP_OFF + (k + 1) * CAP],
                    R2b[:, HP_OFF + k * CAP: HP_OFF + (k + 1) * CAP],
                    R3b[:, sgo:sgo + CAP], op=ALU.mult), inc=sv)
                ps_release(psu, sv)
                silu_free[k % 2] = (sv, sv.n)

        def expert_down_scatter(e, b_i):
            srcs = [d_ed[e, i * 128:(i + 1) * 128, q * 512:(q + 1) * 512]
                    for gh in range(2) for i in range(11) for q in (2 * gh, 2 * gh + 1)]
            pump_d = mk_pump(st_wd, srcs, dbuf_slot)
            pump_d()
            GTb = qt4[:, :].bitcast(BF16)
            gt_tt = 0
            for gh in range(2):
                for i in range(11):
                    for q in (2 * gh, 2 * gh + 1):
                        par, dsem, dval = st_wd.take()
                        W("t", dsem, dval)
                        ql = q - 2 * gh
                        for sc in range(NGC):
                            if i == 0 and ql == 0:
                                ps_wait(PS[sc])
                            T(lambda E, par=par, sc=sc, ql=ql, i=i: E.matmul(
                                PS[sc][:, ql * 512:(ql + 1) * 512],
                                R2b[:, HP_OFF + i * CAP + sc * 128: HP_OFF + i * CAP + (sc + 1) * 128],
                                dbuf[:, par * 512:(par + 1) * 512],
                                start=(i == 0), stop=(i == 10)),
                              inc=(st if sc == NGC - 1 else None))
                        st_wd.consumed(par, st)
                        pump_d()
                W("s", st)
                for sc in range(NGC):
                    A(lambda E, sc=sc, gh=gh: E.copy(
                        R2b[:, XG_OFF + sc * 2048 + gh * 1024: XG_OFF + sc * 2048 + (gh + 1) * 1024],
                        PS[sc][:, :]), inc=sa)
                    ps_release(PS[sc], sa)
                if gh == 0:
                    pDb = psD[:, :].bitcast(BF16)
                    for tt in range(8):
                        rco = (tt % 2) * CAP
                        W("v", st)
                        V(lambda E, tt=tt, e=e, rco=rco: E.tensor_scalar(
                            Rcmp[:, rco:rco + CAP], iotaC[:],
                            slotf[:, tt * NE + e: tt * NE + e + 1], None,
                            op0=ALU.is_equal), inc=sv)
                        W("t", sv)
                        ps_wait(psD)
                        for sc in range(NGC):
                            T(lambda E, tt=tt, sc=sc, rco=rco, pDb=pDb: E.transpose(
                                pDb[:, sc * 128:(sc + 1) * 128],
                                Rcmp[:, rco + sc * 128: rco + (sc + 1) * 128],
                                identb[:]),
                              inc=(st if sc == NGC - 1 else None))
                        W("s", st)
                        dst3 = GTb[:, 0:3 * TB].rearrange("p (sc t) -> p sc t", t=TB)[:, :, tt * 128:(tt + 1) * 128]
                        src3 = pDb[:, 0:NGC * 128].rearrange("p (sc t) -> p sc t", t=128)
                        A(lambda E, d3=dst3, s3=src3: E.copy(d3, s3), inc=sa)
                        ps_release(psD, sa)
            W("t", sa)
            for gq in range(4):
                for gl in range(4):
                    g = gq * 4 + gl
                    ps_wait(PS[gl])
                    for sc in range(NGC):
                        for th in range(2):
                            T(lambda E, g=g, sc=sc, th=th, gl=gl: E.matmul(
                                PS[gl][:, th * 512:(th + 1) * 512],
                                R2b[:, XG_OFF + sc * 2048 + g * 128: XG_OFF + sc * 2048 + (g + 1) * 128],
                                GTb[:, sc * TB + th * 512: sc * TB + (th + 1) * 512],
                                start=(sc == 0), stop=(sc == NGC - 1)),
                              inc=(st if (sc == NGC - 1 and th == 1) else None))
                W("v", st)
                for gl in range(4):
                    g = gq * 4 + gl
                    V(lambda E, g=g, gl=gl: E.tensor_tensor(accv(g, 0, TB), PS[gl][:, :],
                                                            accv(g, 0, TB), op=ALU.add),
                      inc=sv)
                    ps_release(PS[gl], sv)

        def experts(s_i, b_i):
            phase_gate()
            st_g.free[0] = (st, st.n)
            st_g.free[1] = (st, st.n)
            for e in range(NE):
                expert_gather(e, b_i)
                expert_up(e)
                expert_down_scatter(e, b_i)

        def final(s_i, b_i):
            kb = s_i * NBLK_SEQ + b_i
            phase_gate()
            W("y", st); W("y", sv); W("y", sa)
            W("y", s_sc)
            pDb = psD[:, :].bitcast(BF16)
            fin_free = [None, None]
            for tt in range(8):
                gt = kb * 8 + tt
                par = tt % 2
                am = qt4F[:, 0:2048] if par == 0 else obufF[:, 0:2048]
                if fin_free[par] is not None:
                    W("y", fin_free[par][0], fin_free[par][1])
                Y(lambda E, gt=gt, am=am: E.dma_start(out=am, in_=d_x[gt * 128:(gt + 1) * 128, :]),
                  inc=s_fi)
                ld_done = s_fi.n
                src3 = acc[:, :].rearrange("p (g t) -> p g t", t=TB)[:, :, tt * 128:(tt + 1) * 128]
                W("v", st)
                V(lambda E, s3=src3: E.tensor_copy(R3b[:, 0:2048].rearrange(
                    "p (g t) -> p g t", t=128), s3), inc=sv)
                W("t", sv)
                ps_wait(psD)
                for g in range(NF):
                    T(lambda E, g=g, pDb=pDb: E.transpose(
                        pDb[:, g * 128:(g + 1) * 128],
                        R3b[:, g * 128:(g + 1) * 128], identb[:]),
                      inc=(st if g == NF - 1 else None))
                W("v", st)
                W("v", s_fi, ld_done)
                V(lambda E, am=am, pDb=pDb: E.tensor_tensor(am, pDb, am, op=ALU.add), inc=sv)
                ps_release(psD, sv)
                W("y", sv)
                Y(lambda E, gt=gt, am=am: E.dma_start(out=d_out[gt * 128:(gt + 1) * 128, :],
                                                      in_=am), inc=sout)
                fin_free[par] = (sout, sout.n)

        def block(s_i, b_i):
            p0(s_i, b_i)
            if tap == "p0":
                tap_dump("xnT", xnT[:, :])
                tap_dump("ckvT", ckvT[:, :])
                tap_dump("cT", cT[:, :])
                tap_dump("sbc", sbc[:, :])
                tap_dump("slotf", slotf[:, :])
                return True
            attn(s_i, b_i, 0)
            if tap == "attn":
                tap_dump("obuf", obuf[:, :])
                tap_dump("qt4", qt4[:, :])
                tap_dump("ckvT", ckvT[:, :])
                return True
            if tap == "expg":
                shared(s_i, b_i)
                phase_gate()
                st_g.free[0] = (st, st.n)
                st_g.free[1] = (st, st.n)
                expert_gather(0, b_i)
                tap_dump("xgT", R2b[:, XG_OFF:XG_OFF + NF * CAP])
                tap_dump("cbc", cbc[:, :])
                tap_dump("slotf", slotf[:, :])
                tap_dump("mskb", mskb_all[:, :])
                tap_dump("comba", comb_all[:, :])
                return True
            if tap == "exp2":
                shared(s_i, b_i)
                phase_gate()
                st_g.free[0] = (st, st.n)
                st_g.free[1] = (st, st.n)
                expert_gather(0, b_i)
                expert_up(0)
                tap_dump("hp", R2b[:, HP_OFF:HP_OFF + 11 * CAP])
                tap_dump("mskb", mskb_all[:, :])
                tap_dump("comba", comb_all[:, :])
                return True
            if tap == "exp3":
                shared(s_i, b_i)
                phase_gate()
                st_g.free[0] = (st, st.n)
                st_g.free[1] = (st, st.n)
                expert_gather(0, b_i)
                expert_up(0)
                expert_down_scatter(0, b_i)
                tap_dump("accp", acc[:, :])
                tap_dump("mskb", mskb_all[:, :])
                tap_dump("comba", comb_all[:, :])
                return True
            if tap == "exp1":
                shared(s_i, b_i)
                experts(s_i, b_i)
                tap_dump("accp", acc[:, :])
                tap_dump("slotf", slotf[:, :])
                tap_dump("xgT", R2b[:, XG_OFF:XG_OFF + NF * CAP])
                tap_dump("cbc", cbc[:, :])
                return True
            shared(s_i, b_i)
            woadd(s_i, b_i, 0, init=False)
            if tap == "sha":
                tap_dump("accp", acc[:, :])
                return True
            for hg in range(1, 4):
                attn(s_i, b_i, hg)
                woadd(s_i, b_i, hg, init=False)
            experts(s_i, b_i)
            final(s_i, b_i)
            return False

        stop = False
        for s_i in range(SEQ_PC):
            for b_i in range(NBLK_SEQ):
                stop = block(s_i, b_i)
                if stop or tap == "blk1":
                    stop = True
                    break
            if stop:
                break

        W("y", sout)

        blk = eC(nc.Block())
        blk.tensor(lambda E: [f(E) for f in plan.q["t"]] and None)
        blk.vector(lambda E: [f(E) for f in plan.q["v"]] and None)
        blk.scalar(lambda E: [f(E) for f in plan.q["s"]] and None)
        blk.gpsimd(lambda E: [f(E) for f in plan.q["g"]] and None)
        blk.sync(lambda E: [f(E) for f in plan.q["y"]] and None)

    return nc


def _stage_weights(w):
    f32 = lambda a: np.ascontiguousarray(np.asarray(a), dtype=np.float32)
    out = {}
    wdkvT = f32(w["L0_attn_w_dkv"]).T
    out["wdkv_h"] = np.ascontiguousarray(
        wdkvT.reshape(NF, 128, 128).transpose(1, 0, 2).reshape(128, NF * 128))
    out["wq_T"] = f32(w["L0_attn_w_q"]).T.copy()
    out["wuk_h"] = np.ascontiguousarray(
        f32(w["L0_attn_w_uk"]).reshape(NH, 128, 128).transpose(1, 0, 2).reshape(128, NH * 128))
    out["wuv_T"] = f32(w["L0_attn_w_uv"]).T.copy()
    out["wo_T"] = f32(w["L0_attn_w_o"]).T.copy()
    out["gw_h"] = np.ascontiguousarray(
        f32(w["L0_ffn_gate_w"]).T.reshape(NF, 128, NE).transpose(1, 0, 2).reshape(128, NF * NE))
    out["sgv_h"] = np.ascontiguousarray(
        f32(w["L0_ffn_shared_gate_vec"]).reshape(NF, 128).T)
    sguT = f32(w["L0_ffn_shared_gate_up"]).T
    out["sgu_c"] = np.ascontiguousarray(
        sguT.reshape(NF, 128, NI_S * 2, 128).transpose(2, 1, 0, 3).reshape(NI_S * 2, 128, NF * 128))
    out["sd_T"] = f32(w["L0_ffn_shared_down"]).T.copy()
    eguT = f32(w["L0_ffn_expert_gate_up"]).transpose(0, 2, 1)
    out["egu_c"] = np.ascontiguousarray(
        eguT.reshape(NE, NF, 128, NI_E * 2, 128).transpose(0, 3, 2, 1, 4)
        .reshape(NE, NI_E * 2, 128, NF * 128))
    out["ed_T"] = np.ascontiguousarray(f32(w["L0_ffn_expert_down"]).transpose(0, 2, 1))
    p = np.arange(128)[:, None]
    f = np.arange(128)[None, :]
    al = np.empty((128, BAND * 128), np.float32)
    for m in range(BAND):
        k = BAND - 1 - m
        n = 128 * k + p - f
        al[:, m * 128:(m + 1) * 128] = np.where(n >= 0, -0.0625 * n, NEG_BIG)
    out["alibi_c"] = al
    out["ident_c"] = np.eye(128, dtype=np.float32)
    out["ones_c"] = np.ones((1, 128), np.float32)
    out["Lh_c"] = np.triu(np.ones((128, 128), np.float32), 1)
    out["iota_c"] = np.tile(np.arange(CAP, dtype=np.float32)[None, :], (128, 1))
    out["Uh_c"] = np.tril(np.ones((128, 128), np.float32))
    return out


def kernel(**inputs):
    x = np.asarray(inputs["L0_attn_hidden_states"], np.float32)
    shared_w = _stage_weights(inputs)
    nc = build_program()
    in_maps = []
    for c in range(NCORE):
        xc = np.ascontiguousarray(x[SEQ_PC * c: SEQ_PC * (c + 1)].reshape(TOK, D))
        m = dict(shared_w)
        m["x_tok"] = xc
        in_maps.append(m)
    res = run_bass_kernel_spmd(nc, in_maps, core_ids=list(range(NCORE)))
    outs = []
    for c in range(NCORE):
        o = np.asarray(res.results[c]["out_tok"], np.float32)
        outs.append(o.reshape(SEQ_PC, S, D))
    return np.concatenate(outs, axis=0)


# version 05
# speedup vs baseline: 1.1640x; 1.1640x over previous
import numpy as np
from contextlib import ExitStack

import concourse.bass as bass
import concourse.mybir as mybir
from concourse.bass import IndirectOffsetOnAxis
from concourse.bass_utils import run_bass_kernel_spmd

F32 = mybir.dt.float32
BF16 = mybir.dt.bfloat16
I32 = mybir.dt.int32
AF = mybir.ActivationFunctionType
ALU = mybir.AluOpType
AX = mybir.AxisListType

D = 2048
S = 2048
NCORE = 8
SEQ_PC = 2
TOK = SEQ_PC * S
TB = 1024
NBLK_SEQ = S // TB
NF = D // 128
NH = 16
HG = 4
NE = 16
I_E = 1408
NI_E = I_E // 128
SI = 5632
NI_S = SI // 128
SCALE = float(128.0 ** -0.5)
EPS = 1e-5
NEG_BIG = -30000.0
BAND = 9
CAP = 384
NGC = CAP // 128
EHI = 524288.0
OOBF = float(1 << 30)


class Sem:
    __slots__ = ("h", "n", "name")

    def __init__(self, ctx, nc, name):
        self.h = ctx.enter_context(nc.semaphore(name))
        self.n = 0
        self.name = name


class Plan:
    def __init__(self):
        self.q = {k: [] for k in "tvsgy"}

    def add(self, eng, fn):
        self.q[eng].append(fn)

    def wait(self, eng, sem, val=None):
        v = sem.n if val is None else val
        if v <= 0:
            return
        self.q[eng].append(lambda E, h=sem.h, v=v: E.wait_ge(h, v))


class DStream:

    def __init__(self, ctx, nc, name, plan, queue, n_bufs=2):
        self.nb = n_bufs
        self.sems = [Sem(ctx, nc, f"{name}{i}") for i in range(n_bufs)]
        self.free = [None] * n_bufs
        self.idx = 0
        self.consumed_idx = -1
        self.plan = plan
        self.queue = queue
        self.pending = []

    def load(self, dst_ap, src_ap):
        p = self.idx % self.nb
        assert self.idx - self.nb < 0 or self.consumed_idx >= self.idx - self.nb, \
            f"stream over-prefetch: load {self.idx}, consumed {self.consumed_idx}, nb {self.nb}"
        s = self.sems[p]
        fr = self.free[p]

        def thunk(E, fr=fr, dst=dst_ap, src=src_ap, h=s.h):
            if fr is not None:
                E.wait_ge(fr[0].h, fr[1])
            E.dma_start(out=dst, in_=src).then_inc(h, 16)

        self.plan.add(self.queue, thunk)
        s.n += 16
        self.idx += 1
        self.pending.append((p, s, s.n))

    def take(self):
        return self.pending.pop(0)

    def consumed(self, p, sem):
        self.free[p] = (sem, sem.n)
        self.consumed_idx += 1


def build_program(tap=None):
    nc = bass.Bass("TRN2", target_bir_lowering=False)

    d_x = nc.dram_tensor("x_tok", [TOK, D], F32, kind="ExternalInput").ap()
    d_wdkv = nc.dram_tensor("wdkv_h", [128, NF * 128], F32, kind="ExternalInput").ap()
    d_wq = nc.dram_tensor("wq_T", [D, D], F32, kind="ExternalInput").ap()
    d_wuk = nc.dram_tensor("wuk_h", [128, NH * 128], F32, kind="ExternalInput").ap()
    d_wuv = nc.dram_tensor("wuv_T", [128, NH * 128], F32, kind="ExternalInput").ap()
    d_wo = nc.dram_tensor("wo_T", [D, D], F32, kind="ExternalInput").ap()
    d_gw = nc.dram_tensor("gw_h", [128, NF * NE], F32, kind="ExternalInput").ap()
    d_sgv = nc.dram_tensor("sgv_h", [128, NF], F32, kind="ExternalInput").ap()
    d_sgu = nc.dram_tensor("sgu_c", [NI_S * 2, 128, NF * 128], F32, kind="ExternalInput").ap()
    d_sd = nc.dram_tensor("sd_T", [SI, D], F32, kind="ExternalInput").ap()
    d_egu = nc.dram_tensor("egu_c", [NE, NI_E * 2, 128, NF * 128], F32, kind="ExternalInput").ap()
    d_ed = nc.dram_tensor("ed_T", [NE, 2, NI_E, 128, 1024], F32, kind="ExternalInput").ap()
    d_alibi = nc.dram_tensor("alibi_c", [128, BAND * 128], F32, kind="ExternalInput").ap()
    d_ident = nc.dram_tensor("ident_c", [128, 128], F32, kind="ExternalInput").ap()
    d_ones = nc.dram_tensor("ones_c", [1, 128], F32, kind="ExternalInput").ap()
    d_L = nc.dram_tensor("Lh_c", [128, 128], F32, kind="ExternalInput").ap()
    d_U = nc.dram_tensor("Uh_c", [128, 128], F32, kind="ExternalInput").ap()
    d_iota = nc.dram_tensor("iota_c", [128, CAP], F32, kind="ExternalInput").ap()
    d_out = nc.dram_tensor("out_tok", [TOK, D], F32, kind="ExternalOutput").ap()
    d_xntm = nc.dram_tensor("xn_tm", [TB, D], BF16, kind="Internal").ap()

    with ExitStack() as ctx:
        eC = ctx.enter_context

        xnT = eC(nc.sbuf_tensor("xnT", [128, NF * TB], BF16))
        acc = eC(nc.sbuf_tensor("acc", [128, NF * TB], F32))
        ckvT = eC(nc.sbuf_tensor("ckvT", [128, S], BF16))
        ckvtm = eC(nc.sbuf_tensor("ckvtm", [128, NF * 128], BF16))
        obuf = eC(nc.sbuf_tensor("obuf", [128, HG * TB], BF16))
        qt4 = eC(nc.sbuf_tensor("qt4", [128, HG * TB], BF16))
        alibi = eC(nc.sbuf_tensor("alibi", [128, BAND * 128], BF16))
        wdkv = eC(nc.sbuf_tensor("wdkv", [128, NF * 128], BF16))
        wuk = eC(nc.sbuf_tensor("wuk", [128, NH * 128], BF16))
        wuvT = eC(nc.sbuf_tensor("wuvT", [128, NH * 128], BF16))
        gw = eC(nc.sbuf_tensor("gw", [128, NF * NE], BF16))
        sgv = eC(nc.sbuf_tensor("sgv", [128, NF], BF16))
        identb = eC(nc.sbuf_tensor("identb", [128, 128], BF16))
        onesb = eC(nc.sbuf_tensor("onesb", [1, 128], BF16))
        Lb = eC(nc.sbuf_tensor("Lb", [128, 128], BF16))
        Ub = eC(nc.sbuf_tensor("Ub", [128, 128], BF16))
        mskb_all = eC(nc.sbuf_tensor("mskb_all", [128, 128], BF16))
        comb_all = eC(nc.sbuf_tensor("comb_all", [128, 128], BF16))
        slotf = eC(nc.sbuf_tensor("slotf", [128, 128], F32))
        ubuf = eC(nc.sbuf_tensor("ubuf", [128, 2 * 4096], BF16))
        dbuf = eC(nc.sbuf_tensor("dbuf", [128, 3 * 512], BF16))
        wqb = eC(nc.sbuf_tensor("wqb", [128, 3 * 512], BF16))
        dstr2 = eC(nc.sbuf_tensor("dstr2", [128, 4096], BF16))
        R1f = eC(nc.sbuf_tensor("R1f", [128, 2048], F32))
        R2b = eC(nc.sbuf_tensor("R2b", [128, NI_E * TB], BF16))
        R3b = eC(nc.sbuf_tensor("R3b", [128, 2048], BF16))
        cbc = eC(nc.sbuf_tensor("cbc", [128, CAP], BF16))
        sbc = eC(nc.sbuf_tensor("sbc", [128, TB], BF16))
        rowb = eC(nc.sbuf_tensor("rowb", [1, TB], BF16))
        iotaC = eC(nc.sbuf_tensor("iotaC", [128, CAP], F32))
        stats = eC(nc.sbuf_tensor("stats", [128, 12], F32))
        sumr, ssq, negm, vep, lnv, rstd, nmax, sums, rsum, s4, l4, r4 = (
            stats[:, i:i + 1] for i in range(12))
        t8 = eC(nc.sbuf_tensor("t8", [128, 8], F32))
        Esb = eC(nc.sbuf_tensor("Esb", [128, NE], F32))
        msk = eC(nc.sbuf_tensor("msk", [128, NE], F32))
        cw = eC(nc.sbuf_tensor("cw", [128, NE], F32))
        comb = eC(nc.sbuf_tensor("comb", [128, NE], BF16))

        srw = rowb[:, 0:TB]
        crow = rowb[:, 0:CAP]

        psA = eC(nc.psum_tensor("psA", [128, 1024], F32))
        psB = eC(nc.psum_tensor("psB", [128, 1024], F32))
        psC = eC(nc.psum_tensor("psC", [128, 1024], F32))
        psD = eC(nc.psum_tensor("psD", [128, 1024], F32))
        PS = [psA, psB, psC, psD]

        sv = Sem(ctx, nc, "sv")
        sa = Sem(ctx, nc, "sa")
        st = Sem(ctx, nc, "st")
        ssm = Sem(ctx, nc, "ssm")
        sout = Sem(ctx, nc, "sout")
        s_idx = Sem(ctx, nc, "sidx")
        s_xn = Sem(ctx, nc, "sxn")
        s_acm = Sem(ctx, nc, "sacm")
        s_gx = Sem(ctx, nc, "sgx")
        s_sc = Sem(ctx, nc, "ssc")
        s_cv = Sem(ctx, nc, "scv")
        s_fi = Sem(ctx, nc, "sfi")

        plan = Plan()
        W = plan.wait
        st_wu = DStream(ctx, nc, "wu", plan, "g", n_bufs=2)
        st_wd = DStream(ctx, nc, "wd", plan, "g", n_bufs=3)
        st_wq = DStream(ctx, nc, "wq", plan, "g", n_bufs=3)
        st_do = DStream(ctx, nc, "do", plan, "g", n_bufs=2)
        st_x = DStream(ctx, nc, "sx", plan, "y")

        def mk_pump(stream, srcs, dst_fn):
            base = stream.idx
            state = {"next": 0}

            def pump():
                while (state["next"] < len(srcs)
                       and base + state["next"] <= stream.consumed_idx + stream.nb - 1):
                    j = state["next"]
                    stream.load(dst_fn(base + j), srcs[j])
                    state["next"] += 1
            return pump

        ps_free = {}
        silu_free = [None, None]

        def ps_wait(ps):
            fr = ps_free.get(ps.name, None)
            if fr is not None:
                W("t", fr[0], fr[1])

        def ps_release(ps, sem):
            ps_free[ps.name] = (sem, sem.n)

        def ubuf_slot(j):
            return ubuf[:, (j % 2) * 4096:((j % 2) + 1) * 4096].rearrange(
                "p (k c) -> p k c", c=2048)

        def dbuf_slot(j):
            return dbuf[:, (j % 3) * 512:((j % 3) + 1) * 512]

        def dq_slot(j):
            s_ = obuf[:, 0:4096] if j % 2 == 0 else dstr2[:, 0:4096]
            return s_.rearrange("p (k c) -> p k c", c=1024)

        def wqb_slot(j):
            return wqb[:, (j % 3) * 512:((j % 3) + 1) * 512]

        P_OFF, PT_OFF, P2_OFF = 0, 1152, 2304
        SPv = R2b[:, 3456:5760].bitcast(F32)
        QH_OFF = 5760
        HP_OFF, XG_OFF = 0, NI_E * CAP
        R2bF = R2b[:, :].bitcast(F32)
        qt4F = qt4[:, :].bitcast(F32)
        obufF = obuf[:, :].bitcast(F32)

        def T(fn, inc=None, k=1):
            if inc is None:
                plan.add("t", fn)
            else:
                plan.add("t", lambda E, f=fn, h=inc.h, kk=k: f(E).then_inc(h, kk))
                inc.n += k

        def V(fn, inc=None, k=1):
            if inc is None:
                plan.add("v", fn)
            else:
                plan.add("v", lambda E, f=fn, h=inc.h, kk=k: f(E).then_inc(h, kk))
                inc.n += k

        def A(fn, inc=None, k=1):
            if inc is None:
                plan.add("s", fn)
            else:
                plan.add("s", lambda E, f=fn, h=inc.h, kk=k: f(E).then_inc(h, kk))
                inc.n += k

        def G(fn, inc=None, k=16):
            if inc is None:
                plan.add("g", fn)
            else:
                plan.add("g", lambda E, f=fn, h=inc.h, kk=k: f(E).then_inc(h, kk))
                inc.n += k

        def Y(fn, inc=None, k=16):
            if inc is None:
                plan.add("y", fn)
            else:
                plan.add("y", lambda E, f=fn, h=inc.h, kk=k: f(E).then_inc(h, kk))
                inc.n += k

        def phase_gate():
            W("t", sv); W("t", sa)
            W("v", st); W("v", sa)
            W("s", st); W("s", sv)

        def vrt():
            V(lambda E: E.engine_nop(), inc=sv)
            W("s", sv)
            A(lambda E: E.nop(), inc=sa)
            W("v", sa)

        def tap_dump(name, ap):
            if tap is None:
                return
            dr = nc.dram_tensor("tap_" + name, list(ap.shape), ap.dtype,
                                kind="ExternalOutput").ap()
            W("y", sv); W("y", sa); W("y", st)
            Y(lambda E, d=dr, s_=ap: E.dma_start(out=d, in_=s_), inc=sout)

        def xn(g, t0, w):
            return xnT[:, g * TB + t0: g * TB + t0 + w]

        def accv(g, t0, w):
            return acc[:, g * TB + t0: g * TB + t0 + w]

        for dst, src in ((wdkv, d_wdkv), (wuk, d_wuk), (wuvT, d_wuv), (gw, d_gw),
                         (sgv, d_sgv), (identb, d_ident), (onesb, d_ones),
                         (alibi, d_alibi), (Lb, d_L), (Ub, d_U), (iotaC, d_iota)):
            plan.add("g", lambda E, d=dst, s_=src, h=ssm.h:
                     E.dma_start(out=d[:], in_=s_).then_inc(h, 16))
            ssm.n += 16
        for e_ in "tvs":
            W(e_, ssm)

        def xslot(gi):
            return R1f[:, 0:2048] if gi % 2 == 0 else R2bF[:, 0:2048]

        st_x.load(xslot(0), d_x[0:128, :])

        def p0(s_i, b_i):
            kb = s_i * NBLK_SEQ + b_i
            phase_gate()
            W("y", st)
            for tt in range(8):
                gt = kb * 8 + tt
                par, xsem, xval = st_x.take()
                xa = xslot(gt)
                if gt + 1 < TOK // 128:
                    st_x.load(xslot(gt + 1), d_x[(gt + 1) * 128:(gt + 2) * 128, :])
                W("v", xsem, xval)
                V(lambda E, x_=xa: E.reduce_sum(sumr, x_, axis=AX.X), inc=sv)
                W("s", sv)
                A(lambda E: E.nop(), inc=sa)
                W("v", sa)
                V(lambda E: E.tensor_scalar(negm, sumr, -1.0 / D, None, op0=ALU.mult),
                  inc=sv)
                W("s", sv)
                A(lambda E: E.nop(), inc=sa)
                W("v", sa)
                V(lambda E, x_=xa: E.tensor_scalar(R3b[:, 0:2048], x_, negm, None,
                                                   op0=ALU.add), inc=sv)
                st_x.consumed(par, sv)
                W("s", sv)
                A(lambda E: E.activation(R2b[:, 4096:6144], R3b[:, 0:2048], AF.Square,
                                         accum_out=ssq), inc=sa)
                W("v", sa)
                V(lambda E: E.tensor_scalar(vep, ssq, 1.0 / D, EPS,
                                            op0=ALU.mult, op1=ALU.add), inc=sv)
                W("s", sv)
                A(lambda E: E.activation(lnv, vep, AF.Ln), inc=sa)
                W("v", sa)
                V(lambda E: E.engine_nop(), inc=sv)
                W("s", sv)
                A(lambda E: E.activation(rstd, lnv, AF.Exp, scale=-0.5), inc=sa)
                W("v", sa)
                W("v", s_xn)
                V(lambda E: E.tensor_scalar(R2b[:, 6144:8192], R3b[:, 0:2048],
                                            rstd, None, op0=ALU.mult), inc=sv)
                W("y", sv)
                Y(lambda E, tt=tt: E.dma_start(out=d_xntm[tt * 128:(tt + 1) * 128, :],
                                               in_=R2b[:, 6144:8192]), inc=s_xn)
                W("t", sv)
                pdb = psD[:, :].bitcast(BF16)
                for g in range(NF):
                    T(lambda E, g=g, pdb=pdb: E.transpose(
                        pdb[:, g * 128:(g + 1) * 128],
                        R2b[:, 6144 + g * 128: 6144 + (g + 1) * 128], identb[:]),
                      inc=(st if g == NF - 1 else None))
                W("v", st)
                dst3 = xnT[:, :].rearrange("p (g t) -> p g t", t=TB)[:, :, tt * 128:(tt + 1) * 128]
                src3 = pdb.rearrange("p (g t) -> p g t", t=128)
                V(lambda E, d3=dst3, s3=src3: E.tensor_copy(d3, s3), inc=sv)

            W("t", sv)
            boff = b_i * TB
            ps_wait(psA)
            for th in range(2):
                for g in range(NF):
                    T(lambda E, th=th, g=g: E.matmul(
                        psA[:, th * 512:(th + 1) * 512],
                        wdkv[:, g * 128:(g + 1) * 128],
                        xn(g, th * 512, 512),
                        start=(g == 0), stop=(g == NF - 1)),
                      inc=(st if (th == 1 and g == NF - 1) else None))
            W("s", st)
            A(lambda E, bo=boff: E.copy(ckvT[:, bo:bo + TB], psA[:, :]), inc=sa)
            ps_release(psA, sa)
            W("t", sa)
            pdb = psD[:, :].bitcast(BF16)
            for jc in range(8):
                T(lambda E, jc=jc, bo=boff, pdb=pdb: E.transpose(
                    pdb[:, jc * 128:(jc + 1) * 128],
                    ckvT[:, bo + jc * 128: bo + (jc + 1) * 128], identb[:]),
                  inc=(st if jc == 7 else None))
            W("v", st)
            dst3 = ckvtm[:, (b_i * 8) * 128:(b_i * 8 + 8) * 128]
            V(lambda E, d_=dst3, pdb=pdb: E.tensor_copy(d_, pdb[:, 0:1024]), inc=sv)

            psI = psB[:, 512:528]
            for tt in range(8):
                W("t", sv)
                for g in range(NF):
                    T(lambda E, tt=tt, g=g: E.matmul(
                        psB[:, 0:NE],
                        xn(g, tt * 128, 128),
                        gw[:, g * NE:(g + 1) * NE],
                        start=(g == 0), stop=(g == NF - 1)),
                      inc=(st if g == NF - 1 else None))
                W("v", st)
                V(lambda E: E.reduce_max(nmax, psB[:, 0:NE], axis=AX.X, negate=True), inc=sv)
                W("s", sv)
                A(lambda E: E.activation(Esb[:], psB[:, 0:NE], AF.Exp, bias=nmax), inc=sa)
                W("v", sa)
                V(lambda E: E.max(out=t8[:], in_=Esb[:]), inc=sv)
                W("s", sv)
                A(lambda E: E.nop(), inc=sa)
                W("v", sa)
                V(lambda E: E.tensor_scalar(msk[:], Esb[:], t8[:, 3:4], None, op0=ALU.is_ge),
                  inc=sv)
                W("s", sv)
                A(lambda E: E.nop(), inc=sa)
                W("v", sa)
                V(lambda E: E.tensor_mul(cw[:], Esb[:], msk[:]))
                V(lambda E, tt=tt: E.tensor_copy(mskb_all[:, tt * NE:(tt + 1) * NE], msk[:]))
                V(lambda E: E.tensor_scalar(msk[:], msk[:], -4096.0, 4096.0,
                                            op0=ALU.mult, op1=ALU.add), inc=sv)
                W("s", sv)
                A(lambda E: E.nop(), inc=sa)
                W("v", sa)
                V(lambda E: E.reduce_sum(s4, cw[:], axis=AX.X), inc=sv)
                W("s", sv)
                A(lambda E: E.activation(l4, s4, AF.Ln), inc=sa)
                W("v", sa)
                V(lambda E: E.engine_nop(), inc=sv)
                W("s", sv)
                A(lambda E: E.activation(r4, l4, AF.Exp, scale=-1.0), inc=sa)
                W("v", sa)
                V(lambda E: E.tensor_scalar(comb[:], cw[:], r4, None, op0=ALU.mult), inc=sv)
                W("t", sv)
                T(lambda E, tt=tt: E.matmul(psI, Lb[:], mskb_all[:, tt * NE:(tt + 1) * NE],
                                            start=(tt == 0), stop=False), inc=st)
                W("v", st)
                V(lambda E, tt=tt: E.scalar_tensor_tensor(
                    slotf[:, tt * NE:(tt + 1) * NE], psI, 1.0, msk[:],
                    op0=ALU.mult, op1=ALU.add))
                V(lambda E, tt=tt: E.tensor_copy(comb_all[:, tt * NE:(tt + 1) * NE],
                                                 comb[:]), inc=sv)
                W("t", sv)
                T(lambda E, tt=tt: E.matmul(psI, Ub[:], mskb_all[:, tt * NE:(tt + 1) * NE],
                                            start=False, stop=(tt == 7)), inc=st)

            W("t", sv)
            for th in range(2):
                for g in range(NF):
                    T(lambda E, th=th, g=g: E.matmul(
                        psB[0:1, th * 512:(th + 1) * 512],
                        sgv[:, g:g + 1],
                        xn(g, th * 512, 512),
                        start=(g == 0), stop=(g == NF - 1)),
                      inc=(st if (th == 1 and g == NF - 1) else None))
            W("s", st)
            A(lambda E: E.activation(srw, psB[0:1, :], AF.Sigmoid), inc=sa)
            ps_wait(psC)
            W("t", sa)
            for th in range(2):
                T(lambda E, th=th: E.matmul(psC[:, th * 512:(th + 1) * 512], onesb[:],
                                            srw[:, th * 512:(th + 1) * 512],
                                            start=True, stop=True),
                  inc=(st if th == 1 else None))
            W("v", st)
            V(lambda E: E.tensor_copy(sbc[:], psC[:, :]), inc=sv)
            ps_release(psC, sv)
            ps_release(psB, sv)
            ps_release(psD, sv)

        def attn(s_i, b_i, hg):
            phase_gate()
            if hg == 0:
                W("s", s_acm)
                W("v", s_acm)
            pump_q = mk_pump(st_wq,
                             [d_wq[f * 128:(f + 1) * 128, hg * 512:(hg + 1) * 512]
                              for f in range(NF)],
                             wqb_slot)
            pump_q()
            for f in range(NF):
                par, qsem, qval = st_wq.take()
                W("t", qsem, qval)
                if f == 0:
                    for p_ in PS:
                        ps_wait(p_)
                for hh in range(HG):
                    for th in range(2):
                        T(lambda E, f=f, hh=hh, th=th, par=par: E.matmul(
                            PS[hh][:, th * 512:(th + 1) * 512],
                            wqb[:, par * 512 + hh * 128: par * 512 + (hh + 1) * 128],
                            xn(f, th * 512, 512),
                            start=(f == 0), stop=(f == NF - 1)),
                          inc=(st if (hh == HG - 1 and th == 1) else None))
                st_wq.consumed(par, st)
                pump_q()
            W("s", st)
            for hh in range(HG):
                A(lambda E, hh=hh: E.copy(R2b[:, QH_OFF + hh * TB: QH_OFF + (hh + 1) * TB],
                                          PS[hh][:, :]), inc=(sa if hh == HG - 1 else None))
            W("t", sa)
            for hh in range(HG):
                h = hg * HG + hh
                for th in range(2):
                    T(lambda E, hh=hh, h=h, th=th: E.matmul(
                        PS[hh][:, th * 512:(th + 1) * 512],
                        wuk[:, h * 128:(h + 1) * 128],
                        R2b[:, QH_OFF + hh * TB + th * 512: QH_OFF + hh * TB + (th + 1) * 512],
                        start=True, stop=True),
                      inc=(st if (hh == HG - 1 and th == 1) else None))
            W("s", st)
            for hh in range(HG):
                A(lambda E, hh=hh: E.activation(qt4[:, hh * TB:(hh + 1) * TB], PS[hh][:, :],
                                                AF.Identity, scale=SCALE),
                  inc=(sa if hh == HG - 1 else None))
            W("t", sa); W("v", sa)

            pBb = psB[:, :].bitcast(BF16)
            for qt in range(8):
                Qg = b_i * 8 + qt
                js = max(0, Qg - (BAND - 1))
                j0b = js * 128
                wband = (Qg + 1) * 128 - j0b
                for hh in range(HG):
                    h = hg * HG + hh
                    nchunks = (wband + 511) // 512
                    for jc5 in range(nchunks):
                        jo = jc5 * 512
                        w_ = min(512, wband - jo)
                        T(lambda E, hh=hh, qt=qt, jo=jo, w_=w_, p_=(jc5 % 2), j0b=j0b: E.matmul(
                            psA[:, p_ * 512: p_ * 512 + w_],
                            qt4[:, hh * TB + qt * 128: hh * TB + (qt + 1) * 128],
                            ckvT[:, j0b + jo: j0b + jo + w_],
                            start=True, stop=True), inc=st)
                        W("v", st)
                        V(lambda E, jo=jo, w_=w_, p_=(jc5 % 2), wb=wband: E.tensor_tensor(
                            SPv[:, jo: jo + w_],
                            psA[:, p_ * 512: p_ * 512 + w_],
                            alibi[:, BAND * 128 - wb + jo: BAND * 128 - wb + jo + w_],
                            op=ALU.add), inc=sv)
                        W("t", sv)
                    V(lambda E, wb=wband: E.reduce_max(nmax, SPv[:, 0:wb],
                                                       axis=AX.X, negate=True), inc=sv)
                    W("s", sv)
                    A(lambda E, wb=wband: E.activation(R2b[:, P_OFF:P_OFF + wb],
                                                       SPv[:, 0:wb], AF.Exp,
                                                       bias=nmax, accum_out=sums), inc=sa)
                    W("v", sa)
                    V(lambda E: E.reciprocal(rsum, sums), inc=sv)
                    W("s", sv)
                    A(lambda E: E.nop(), inc=sa)
                    W("v", sa)
                    V(lambda E, wb=wband: E.tensor_scalar(R2b[:, P2_OFF:P2_OFF + wb],
                                                          R2b[:, P_OFF:P_OFF + wb],
                                                          rsum, None, op0=ALU.mult),
                      inc=sv)
                    W("t", sv)
                    nj = wband // 128
                    for jc in range(nj):
                        T(lambda E, jc=jc: E.transpose(
                            pBb[:, jc * 128:(jc + 1) * 128],
                            R2b[:, P2_OFF + jc * 128: P2_OFF + (jc + 1) * 128],
                            identb[:]),
                          inc=(st if jc == nj - 1 else None))
                    W("s", st)
                    A(lambda E, wb=wband: E.copy(R2b[:, PT_OFF:PT_OFF + wb], pBb[:, 0:wb]),
                      inc=sa)
                    W("t", sa)
                    for jc in range(nj):
                        T(lambda E, jc=jc, js=js, nj=nj: E.matmul(
                            psC[:, 0:128],
                            ckvtm[:, (js + jc) * 128:(js + jc + 1) * 128],
                            R2b[:, PT_OFF + jc * 128: PT_OFF + (jc + 1) * 128],
                            start=(jc == 0), stop=(jc == nj - 1)),
                          inc=(st if jc == nj - 1 else None))
                    W("v", st)
                    V(lambda E: E.tensor_copy(R3b[:, 0:128], psC[:, 0:128]), inc=sv)
                    W("t", sv)
                    T(lambda E, h=h: E.matmul(psC[:, 512:640],
                                              wuvT[:, h * 128:(h + 1) * 128],
                                              R3b[:, 0:128], start=True, stop=True), inc=st)
                    W("s", st)
                    A(lambda E, hh=hh, qt=qt: E.copy(
                        obuf[:, hh * TB + qt * 128: hh * TB + (qt + 1) * 128],
                        psC[:, 512:640]), inc=sa)
            W("v", sa)
            V(lambda E: E.engine_nop(), inc=sv)
            for p_ in PS:
                ps_release(p_, sv)

        def woadd(s_i, b_i, hg, init):
            phase_gate()
            pump_wo = mk_pump(st_wd,
                              [d_wo[(hg * HG + hh) * 128:(hg * HG + hh + 1) * 128,
                                    gq * 512:(gq + 1) * 512]
                               for gq in range(4) for hh in range(HG)],
                              dbuf_slot)
            pump_wo()
            for gq in range(4):
                for hh in range(HG):
                    par, dsem, dval = st_wd.take()
                    W("t", dsem, dval)
                    for gl in range(4):
                        if hh == 0:
                            ps_wait(PS[gl])
                        for th in range(2):
                            T(lambda E, par=par, gl=gl, th=th, hh=hh: E.matmul(
                                PS[gl][:, th * 512:(th + 1) * 512],
                                dbuf[:, par * 512 + gl * 128: par * 512 + (gl + 1) * 128],
                                obuf[:, hh * TB + th * 512: hh * TB + (th + 1) * 512],
                                start=(hh == 0), stop=(hh == HG - 1)),
                              inc=(st if (gl == 3 and th == 1) else None))
                    st_wd.consumed(par, st)
                    pump_wo()
                W("v", st)
                for gl in range(4):
                    g = gq * 4 + gl
                    if init:
                        V(lambda E, g=g, gl=gl: E.tensor_copy(accv(g, 0, TB), PS[gl][:, :]),
                          inc=sv)
                    else:
                        V(lambda E, g=g, gl=gl: E.tensor_tensor(accv(g, 0, TB), PS[gl][:, :],
                                                                accv(g, 0, TB), op=ALU.add),
                          inc=sv)
                    ps_release(PS[gl], sv)

        def ffn_up_pairs_dense(npairs, bcast_sb, srcs):
            pump_u = mk_pump(st_wu, srcs, ubuf_slot)
            pump_u()
            for k in range(npairs):
                psg = PS[(2 * k) % 4]
                psu = PS[(2 * k + 1) % 4]
                sgo = (k % 2) * TB
                par, usem, uval = st_wu.take()
                W("t", usem, uval)
                ps_wait(psg)
                for g in range(NF):
                    for th in range(2):
                        T(lambda E, par=par, g=g, th=th, ps_=psg: E.matmul(
                            ps_[:, th * 512:(th + 1) * 512],
                            ubuf[:, par * 4096 + g * 128: par * 4096 + (g + 1) * 128],
                            xn(g, th * 512, 512),
                            start=(g == 0), stop=(g == NF - 1)),
                          inc=(st if (g == NF - 1 and th == 1) else None))
                g_done = st.n
                ps_wait(psu)
                for g in range(NF):
                    for th in range(2):
                        T(lambda E, par=par, g=g, th=th, ps_=psu: E.matmul(
                            ps_[:, th * 512:(th + 1) * 512],
                            ubuf[:, par * 4096 + 2048 + g * 128: par * 4096 + 2048 + (g + 1) * 128],
                            xn(g, th * 512, 512),
                            start=(g == 0), stop=(g == NF - 1)),
                          inc=(st if (g == NF - 1 and th == 1) else None))
                u_done = st.n
                st_wu.consumed(par, st)
                pump_u()
                W("s", st, g_done)
                if silu_free[k % 2] is not None:
                    W("s", silu_free[k % 2][0], silu_free[k % 2][1])
                A(lambda E, ps_=psg, sgo=sgo: E.activation(R3b[:, sgo:sgo + TB],
                                                           ps_[:, :], AF.Silu), inc=sa)
                ps_release(psg, sa)
                W("v", st, u_done); W("v", sa)
                V(lambda E, ps_=psu, bc=bcast_sb, k=k: E.tensor_tensor(
                    R2b[:, k * TB:(k + 1) * TB], ps_[:, :], bc[:], op=ALU.mult))
                V(lambda E, k=k, sgo=sgo: E.tensor_tensor(
                    R2b[:, k * TB:(k + 1) * TB], R2b[:, k * TB:(k + 1) * TB],
                    R3b[:, sgo:sgo + TB], op=ALU.mult), inc=sv)
                ps_release(psu, sv)
                silu_free[k % 2] = (sv, sv.n)

        def ffn_down_dense(nchunks, slab_fn, init):
            pump_d = mk_pump(st_wd,
                             [slab_fn(i, gq) for gq in range(4) for i in range(nchunks)],
                             dbuf_slot)
            pump_d()
            for gq in range(4):
                for i in range(nchunks):
                    par, dsem, dval = st_wd.take()
                    W("t", dsem, dval)
                    for gl in range(4):
                        if i == 0:
                            ps_wait(PS[gl])
                        for th in range(2):
                            T(lambda E, par=par, gl=gl, th=th, i=i: E.matmul(
                                PS[gl][:, th * 512:(th + 1) * 512],
                                dbuf[:, par * 512 + gl * 128: par * 512 + (gl + 1) * 128],
                                R2b[:, i * TB + th * 512: i * TB + (th + 1) * 512],
                                start=(i == 0), stop=(i == nchunks - 1)),
                              inc=(st if (gl == 3 and th == 1) else None))
                    st_wd.consumed(par, st)
                    pump_d()
                W("v", st)
                for gl in range(4):
                    g = gq * 4 + gl
                    if init:
                        V(lambda E, g=g, gl=gl: E.tensor_copy(accv(g, 0, TB), PS[gl][:, :]),
                          inc=sv)
                    else:
                        V(lambda E, g=g, gl=gl: E.tensor_tensor(accv(g, 0, TB), PS[gl][:, :],
                                                                accv(g, 0, TB), op=ALU.add),
                          inc=sv)
                    ps_release(PS[gl], sv)

        def shared(s_i, b_i):
            phase_gate()
            for gr in range(4):
                srcs = [d_sgu[2 * (gr * 11 + k):2 * (gr * 11 + k) + 2, :, :]
                        .rearrange("k p c -> p k c") for k in range(11)]
                ffn_up_pairs_dense(11, sbc, srcs)
                ffn_down_dense(11, lambda i, gq, gr=gr:
                               d_sd[(gr * 11 + i) * 128:(gr * 11 + i + 1) * 128,
                                    gq * 512:(gq + 1) * 512],
                               init=(gr == 0))

        def expert_gather(e, b_i):
            W("v", st)
            for tt in range(8):
                V(lambda E, tt=tt, e=e: E.tensor_scalar(
                    qt4[:, tt * CAP:(tt + 1) * CAP], iotaC[:],
                    slotf[:, tt * NE + e: tt * NE + e + 1], None,
                    op0=ALU.is_equal), inc=(sv if tt == 7 else None))
            gdst = [psA[:, 0:CAP], psA[:, 512:512 + CAP],
                    psB[:, 0:CAP], psB[:, 512:512 + CAP],
                    psC[:, 0:CAP], psC[:, 512:512 + CAP],
                    psD[:, 0:CAP], psD[:, 512:512 + CAP]]
            W("t", sv)
            W("t", s_gx)
            for half in range(2):
                for p_ in PS:
                    ps_wait(p_)
                for tt in range(8):
                    for fc8 in range(8):
                        fc = half * 8 + fc8
                        T(lambda E, tt=tt, fc=fc, d_=gdst[fc8]: E.matmul(
                            d_, xnT[:, tt * 2048 + fc * 128: tt * 2048 + (fc + 1) * 128],
                            qt4[:, tt * CAP:(tt + 1) * CAP],
                            start=(tt == 0), stop=(tt == 7)),
                          inc=(st if (tt == 7 and fc8 == 7) else None))
                W("s", st)
                for fc8 in range(8):
                    fc = half * 8 + fc8
                    A(lambda E, fc=fc, s_=gdst[fc8]: E.copy(
                        R2b[:, XG_OFF + fc * CAP: XG_OFF + (fc + 1) * CAP], s_),
                      inc=(sa if fc8 == 7 else None))
                for p_ in PS:
                    ps_release(p_, sa)
            ps_wait(psA)
            for tt in range(8):
                T(lambda E, tt=tt, e=e: E.matmul(
                    psA[0:1, 0:CAP],
                    comb_all[:, tt * NE + e: tt * NE + e + 1],
                    qt4[:, tt * CAP:(tt + 1) * CAP],
                    start=(tt == 0), stop=(tt == 7)),
                  inc=(st if tt == 7 else None))
            W("v", st)
            V(lambda E: E.tensor_copy(crow, psA[0:1, 0:CAP]), inc=sv)
            ps_release(psA, sv)
            W("t", sv)
            ps_wait(psD)
            T(lambda E: E.matmul(psD[:, 0:CAP], onesb[:], crow, start=True, stop=True),
              inc=st)
            W("v", st)
            V(lambda E: E.tensor_copy(cbc[:, 0:CAP], psD[:, 0:CAP]), inc=sv)
            ps_release(psD, sv)

        def expert_up(e):
            srcs = [d_egu[e, 2 * k:2 * k + 2, :, :].rearrange("k p c -> p k c")
                    for k in range(11)]
            pump_u = mk_pump(st_wu, srcs, ubuf_slot)
            pump_u()
            W("t", sa)
            for k in range(11):
                psg = [psA, psC][k % 2]
                psu = [psB, psD][k % 2]
                sgo = (k % 2) * TB
                par, usem, uval = st_wu.take()
                W("t", usem, uval)
                ps_wait(psg)
                for c in range(NF):
                    T(lambda E, par=par, c=c, ps_=psg: E.matmul(
                        ps_[:, 0:CAP],
                        ubuf[:, par * 4096 + c * 128: par * 4096 + (c + 1) * 128],
                        R2b[:, XG_OFF + c * CAP: XG_OFF + (c + 1) * CAP],
                        start=(c == 0), stop=(c == NF - 1)),
                      inc=(st if c == NF - 1 else None))
                g_done = st.n
                ps_wait(psu)
                for c in range(NF):
                    T(lambda E, par=par, c=c, ps_=psu: E.matmul(
                        ps_[:, 0:CAP],
                        ubuf[:, par * 4096 + 2048 + c * 128: par * 4096 + 2048 + (c + 1) * 128],
                        R2b[:, XG_OFF + c * CAP: XG_OFF + (c + 1) * CAP],
                        start=(c == 0), stop=(c == NF - 1)),
                      inc=(st if c == NF - 1 else None))
                u_done = st.n
                st_wu.consumed(par, st)
                pump_u()
                W("s", st, g_done)
                if silu_free[k % 2] is not None:
                    W("s", silu_free[k % 2][0], silu_free[k % 2][1])
                A(lambda E, ps_=psg, sgo=sgo: E.activation(R3b[:, sgo:sgo + CAP],
                                                           ps_[:, 0:CAP], AF.Silu), inc=sa)
                ps_release(psg, sa)
                W("v", st, u_done); W("v", sa)
                V(lambda E, ps_=psu, k=k: E.tensor_tensor(
                    R2b[:, HP_OFF + k * CAP: HP_OFF + (k + 1) * CAP],
                    ps_[:, 0:CAP], cbc[:, 0:CAP], op=ALU.mult))
                V(lambda E, k=k, sgo=sgo: E.tensor_tensor(
                    R2b[:, HP_OFF + k * CAP: HP_OFF + (k + 1) * CAP],
                    R2b[:, HP_OFF + k * CAP: HP_OFF + (k + 1) * CAP],
                    R3b[:, sgo:sgo + CAP], op=ALU.mult), inc=sv)
                ps_release(psu, sv)
                silu_free[k % 2] = (sv, sv.n)

        def expert_down_scatter(e, b_i):
            quads = ((0, 4), (4, 4), (8, 3))
            srcs = [d_ed[e, gh, qa:qa + ln, :, :].rearrange("k p c -> p k c")
                    for gh in range(2) for qa, ln in quads]
            pump_d = mk_pump(st_do, srcs,
                             lambda j: dq_slot(j)[:, 0:quads[j % 3][1], :])
            pump_d()
            GTb = qt4[:, :]
            for gh in range(2):
                for qa, ln in quads:
                    par, dsem, dval = st_do.take()
                    slab = obuf[:, 0:4096] if par == 0 else dstr2[:, 0:4096]
                    W("t", dsem, dval)
                    for ii in range(ln):
                        i = qa + ii
                        for sc in range(NGC):
                            if i == 0:
                                ps_wait(PS[sc])
                            for ql in range(2):
                                T(lambda E, slab=slab, ii=ii, sc=sc, ql=ql, i=i: E.matmul(
                                    PS[sc][:, ql * 512:(ql + 1) * 512],
                                    R2b[:, HP_OFF + i * CAP + sc * 128: HP_OFF + i * CAP + (sc + 1) * 128],
                                    slab[:, ii * 1024 + ql * 512: ii * 1024 + (ql + 1) * 512],
                                    start=(i == 0), stop=(i == 10)),
                                  inc=(st if (sc == NGC - 1 and ql == 1) else None))
                    st_do.consumed(par, st)
                    pump_d()
                W("s", st)
                for sc in range(NGC):
                    A(lambda E, sc=sc, gh=gh: E.copy(
                        R2b[:, XG_OFF + sc * 2048 + gh * 1024: XG_OFF + sc * 2048 + (gh + 1) * 1024],
                        PS[sc][:, :]), inc=sa)
                    ps_release(PS[sc], sa)
                if gh == 0:
                    ps_wait(psD)
                    W("v", sa)
                    evac_done = [0, 0]
                    for tt in range(8):
                        rco = (tt % 2) * CAP
                        bko = (tt % 2) * 1024
                        pDb = psD[:, :].bitcast(BF16)
                        W("v", st)
                        V(lambda E, tt=tt, e=e, rco=rco: E.tensor_scalar(
                            R3b[:, rco:rco + CAP], iotaC[:],
                            slotf[:, tt * NE + e: tt * NE + e + 1], None,
                            op0=ALU.is_equal), inc=sv)
                        W("t", sv)
                        if evac_done[tt % 2]:
                            W("t", sa, evac_done[tt % 2])
                        for sc in range(NGC):
                            T(lambda E, sc=sc, rco=rco, bko=bko, pDb=pDb: E.transpose(
                                pDb[:, bko + sc * 128: bko + (sc + 1) * 128],
                                R3b[:, rco + sc * 128: rco + (sc + 1) * 128],
                                identb[:]),
                              inc=(st if sc == NGC - 1 else None))
                        W("s", st)
                        dst3 = GTb[:, 0:3 * TB].rearrange("p (sc t) -> p sc t", t=TB)[:, :, tt * 128:(tt + 1) * 128]
                        src3 = pDb[:, bko:bko + NGC * 128].rearrange("p (sc t) -> p sc t", t=128)
                        A(lambda E, d3=dst3, s3=src3: E.copy(d3, s3), inc=sa)
                        evac_done[tt % 2] = sa.n
                    ps_release(psD, sa)
            W("t", sa)
            for gq in range(4):
                for gl in range(4):
                    g = gq * 4 + gl
                    ps_wait(PS[gl])
                    for sc in range(NGC):
                        for th in range(2):
                            T(lambda E, g=g, sc=sc, th=th, gl=gl: E.matmul(
                                PS[gl][:, th * 512:(th + 1) * 512],
                                R2b[:, XG_OFF + sc * 2048 + g * 128: XG_OFF + sc * 2048 + (g + 1) * 128],
                                GTb[:, sc * TB + th * 512: sc * TB + (th + 1) * 512],
                                start=(sc == 0), stop=(sc == NGC - 1)),
                              inc=(st if (sc == NGC - 1 and th == 1) else None))
                W("v", st)
                for gl in range(4):
                    g = gq * 4 + gl
                    V(lambda E, g=g, gl=gl: E.tensor_tensor(accv(g, 0, TB), PS[gl][:, :],
                                                            accv(g, 0, TB), op=ALU.add),
                      inc=sv)
                    ps_release(PS[gl], sv)

        def load_tiles():
            phase_gate()
            st_do.free[0] = (st, st.n)
            st_do.free[1] = (st, st.n)
            W("y", st)
            W("y", s_xn)
            for tt in range(8):
                Y(lambda E, tt=tt: E.dma_start(
                    out=xnT[:, tt * 2048:(tt + 1) * 2048],
                    in_=d_xntm[tt * 128:(tt + 1) * 128, :]), inc=s_gx)

        def experts(s_i, b_i):
            load_tiles()
            for e in range(NE):
                expert_gather(e, b_i)
                expert_up(e)
                expert_down_scatter(e, b_i)

        def final(s_i, b_i):
            kb = s_i * NBLK_SEQ + b_i
            phase_gate()
            W("y", st); W("y", sv); W("y", sa)
            W("y", s_sc)
            pDb = psD[:, :].bitcast(BF16)
            fin_free = [None, None]
            for tt in range(8):
                gt = kb * 8 + tt
                par = tt % 2
                am = qt4F[:, 0:2048] if par == 0 else obufF[:, 0:2048]
                if fin_free[par] is not None:
                    W("y", fin_free[par][0], fin_free[par][1])
                Y(lambda E, gt=gt, am=am: E.dma_start(out=am, in_=d_x[gt * 128:(gt + 1) * 128, :]),
                  inc=s_fi)
                ld_done = s_fi.n
                src3 = acc[:, :].rearrange("p (g t) -> p g t", t=TB)[:, :, tt * 128:(tt + 1) * 128]
                W("v", st)
                V(lambda E, s3=src3: E.tensor_copy(R3b[:, 0:2048].rearrange(
                    "p (g t) -> p g t", t=128), s3), inc=sv)
                W("t", sv)
                ps_wait(psD)
                for g in range(NF):
                    T(lambda E, g=g, pDb=pDb: E.transpose(
                        pDb[:, g * 128:(g + 1) * 128],
                        R3b[:, g * 128:(g + 1) * 128], identb[:]),
                      inc=(st if g == NF - 1 else None))
                W("v", st)
                W("v", s_fi, ld_done)
                V(lambda E, am=am, pDb=pDb: E.tensor_tensor(am, pDb, am, op=ALU.add), inc=sv)
                ps_release(psD, sv)
                W("y", sv)
                Y(lambda E, gt=gt, am=am: E.dma_start(out=d_out[gt * 128:(gt + 1) * 128, :],
                                                      in_=am), inc=sout)
                fin_free[par] = (sout, sout.n)

        def block(s_i, b_i):
            p0(s_i, b_i)
            if tap == "p0":
                tap_dump("xnT", xnT[:, :])
                tap_dump("ckvT", ckvT[:, :])
                tap_dump("sbc", sbc[:, :])
                tap_dump("slotf", slotf[:, :])
                return True
            attn(s_i, b_i, 0)
            if tap == "attn":
                tap_dump("obuf", obuf[:, :])
                tap_dump("qt4", qt4[:, :])
                tap_dump("ckvT", ckvT[:, :])
                return True
            if tap == "expg":
                shared(s_i, b_i)
                load_tiles()
                expert_gather(0, b_i)
                tap_dump("xgT", R2b[:, XG_OFF:XG_OFF + NF * CAP])
                tap_dump("cbc", cbc[:, :])
                tap_dump("slotf", slotf[:, :])
                tap_dump("mskb", mskb_all[:, :])
                tap_dump("comba", comb_all[:, :])
                return True
            if tap == "exp2":
                shared(s_i, b_i)
                load_tiles()
                expert_gather(0, b_i)
                expert_up(0)
                tap_dump("hp", R2b[:, HP_OFF:HP_OFF + 11 * CAP])
                tap_dump("mskb", mskb_all[:, :])
                tap_dump("comba", comb_all[:, :])
                return True
            if tap == "exp3":
                shared(s_i, b_i)
                load_tiles()
                expert_gather(0, b_i)
                expert_up(0)
                expert_down_scatter(0, b_i)
                tap_dump("accp", acc[:, :])
                tap_dump("mskb", mskb_all[:, :])
                tap_dump("comba", comb_all[:, :])
                return True
            if tap == "exp1":
                shared(s_i, b_i)
                experts(s_i, b_i)
                tap_dump("accp", acc[:, :])
                tap_dump("slotf", slotf[:, :])
                tap_dump("xgT", R2b[:, XG_OFF:XG_OFF + NF * CAP])
                tap_dump("cbc", cbc[:, :])
                return True
            shared(s_i, b_i)
            woadd(s_i, b_i, 0, init=False)
            if tap == "sha":
                tap_dump("accp", acc[:, :])
                return True
            for hg in range(1, 4):
                attn(s_i, b_i, hg)
                woadd(s_i, b_i, hg, init=False)
            experts(s_i, b_i)
            final(s_i, b_i)
            return False

        stop = False
        for s_i in range(SEQ_PC):
            for b_i in range(NBLK_SEQ):
                stop = block(s_i, b_i)
                if stop or tap == "blk1":
                    stop = True
                    break
            if stop:
                break

        W("y", sout)

        blk = eC(nc.Block())
        blk.tensor(lambda E: [f(E) for f in plan.q["t"]] and None)
        blk.vector(lambda E: [f(E) for f in plan.q["v"]] and None)
        blk.scalar(lambda E: [f(E) for f in plan.q["s"]] and None)
        blk.gpsimd(lambda E: [f(E) for f in plan.q["g"]] and None)
        blk.sync(lambda E: [f(E) for f in plan.q["y"]] and None)

    return nc


def _stage_weights(w):
    f32 = lambda a: np.ascontiguousarray(np.asarray(a), dtype=np.float32)
    out = {}
    wdkvT = f32(w["L0_attn_w_dkv"]).T
    out["wdkv_h"] = np.ascontiguousarray(
        wdkvT.reshape(NF, 128, 128).transpose(1, 0, 2).reshape(128, NF * 128))
    out["wq_T"] = f32(w["L0_attn_w_q"]).T.copy()
    out["wuk_h"] = np.ascontiguousarray(
        f32(w["L0_attn_w_uk"]).reshape(NH, 128, 128).transpose(1, 0, 2).reshape(128, NH * 128))
    out["wuv_T"] = f32(w["L0_attn_w_uv"]).T.copy()
    out["wo_T"] = f32(w["L0_attn_w_o"]).T.copy()
    out["gw_h"] = np.ascontiguousarray(
        f32(w["L0_ffn_gate_w"]).T.reshape(NF, 128, NE).transpose(1, 0, 2).reshape(128, NF * NE))
    out["sgv_h"] = np.ascontiguousarray(
        f32(w["L0_ffn_shared_gate_vec"]).reshape(NF, 128).T)
    sguT = f32(w["L0_ffn_shared_gate_up"]).T
    sgu_c = sguT.reshape(NF, 128, NI_S * 2, 128).transpose(2, 1, 0, 3).reshape(NI_S * 2, 128, NF * 128)
    s_ord = []
    for j in range(NI_S):
        s_ord += [j, NI_S + j]
    out["sgu_c"] = np.ascontiguousarray(sgu_c[s_ord])
    out["sd_T"] = f32(w["L0_ffn_shared_down"]).T.copy()
    eguT = f32(w["L0_ffn_expert_gate_up"]).transpose(0, 2, 1)
    egu_c = (eguT.reshape(NE, NF, 128, NI_E * 2, 128).transpose(0, 3, 2, 1, 4)
             .reshape(NE, NI_E * 2, 128, NF * 128))
    e_ord = []
    for j in range(NI_E):
        e_ord += [j, NI_E + j]
    out["egu_c"] = np.ascontiguousarray(egu_c[:, e_ord])
    edT = f32(w["L0_ffn_expert_down"]).transpose(0, 2, 1)
    out["ed_T"] = np.ascontiguousarray(
        edT.reshape(NE, NI_E, 128, 2, 1024).transpose(0, 3, 1, 2, 4))
    p = np.arange(128)[:, None]
    f = np.arange(128)[None, :]
    al = np.empty((128, BAND * 128), np.float32)
    for m in range(BAND):
        k = BAND - 1 - m
        n = 128 * k + p - f
        al[:, m * 128:(m + 1) * 128] = np.where(n >= 0, -0.0625 * n, NEG_BIG)
    out["alibi_c"] = al
    out["ident_c"] = np.eye(128, dtype=np.float32)
    out["ones_c"] = np.ones((1, 128), np.float32)
    out["Lh_c"] = np.triu(np.ones((128, 128), np.float32), 1)
    out["iota_c"] = np.tile(np.arange(CAP, dtype=np.float32)[None, :], (128, 1))
    out["Uh_c"] = np.tril(np.ones((128, 128), np.float32))
    return out


def kernel(**inputs):
    x = np.asarray(inputs["L0_attn_hidden_states"], np.float32)
    shared_w = _stage_weights(inputs)
    nc = build_program()
    in_maps = []
    for c in range(NCORE):
        xc = np.ascontiguousarray(x[SEQ_PC * c: SEQ_PC * (c + 1)].reshape(TOK, D))
        m = dict(shared_w)
        m["x_tok"] = xc
        in_maps.append(m)
    res = run_bass_kernel_spmd(nc, in_maps, core_ids=list(range(NCORE)))
    outs = []
    for c in range(NCORE):
        o = np.asarray(res.results[c]["out_tok"], np.float32)
        outs.append(o.reshape(SEQ_PC, S, D))
    return np.concatenate(outs, axis=0)


# version 06 (final)
# speedup vs baseline: 1.1651x; 1.0010x over previous
import numpy as np
from contextlib import ExitStack

import concourse.bass as bass
import concourse.mybir as mybir
from concourse.bass import IndirectOffsetOnAxis
from concourse.bass_utils import run_bass_kernel_spmd

F32 = mybir.dt.float32
BF16 = mybir.dt.bfloat16
I32 = mybir.dt.int32
AF = mybir.ActivationFunctionType
ALU = mybir.AluOpType
AX = mybir.AxisListType

D = 2048
S = 2048
NCORE = 8
SEQ_PC = 2
TOK = SEQ_PC * S
TB = 1024
NBLK_SEQ = S // TB
NF = D // 128
NH = 16
HG = 4
NE = 16
I_E = 1408
NI_E = I_E // 128
SI = 5632
NI_S = SI // 128
SCALE = float(128.0 ** -0.5)
EPS = 1e-5
NEG_BIG = -30000.0
BAND = 9
CAP = 384
NGC = CAP // 128
EHI = 524288.0
OOBF = float(1 << 30)


class Sem:
    __slots__ = ("h", "n", "name")

    def __init__(self, ctx, nc, name):
        self.h = ctx.enter_context(nc.semaphore(name))
        self.n = 0
        self.name = name


class Plan:
    def __init__(self):
        self.q = {k: [] for k in "tvsgy"}

    def add(self, eng, fn):
        self.q[eng].append(fn)

    def wait(self, eng, sem, val=None):
        v = sem.n if val is None else val
        if v <= 0:
            return
        self.q[eng].append(lambda E, h=sem.h, v=v: E.wait_ge(h, v))


class DStream:

    def __init__(self, ctx, nc, name, plan, queue, n_bufs=2):
        self.nb = n_bufs
        self.sems = [Sem(ctx, nc, f"{name}{i}") for i in range(n_bufs)]
        self.free = [None] * n_bufs
        self.idx = 0
        self.consumed_idx = -1
        self.plan = plan
        self.queue = queue
        self.pending = []

    def load(self, dst_ap, src_ap):
        p = self.idx % self.nb
        assert self.idx - self.nb < 0 or self.consumed_idx >= self.idx - self.nb, \
            f"stream over-prefetch: load {self.idx}, consumed {self.consumed_idx}, nb {self.nb}"
        s = self.sems[p]
        fr = self.free[p]

        def thunk(E, fr=fr, dst=dst_ap, src=src_ap, h=s.h):
            if fr is not None:
                E.wait_ge(fr[0].h, fr[1])
            E.dma_start(out=dst, in_=src).then_inc(h, 16)

        self.plan.add(self.queue, thunk)
        s.n += 16
        self.idx += 1
        self.pending.append((p, s, s.n))

    def take(self):
        return self.pending.pop(0)

    def consumed(self, p, sem):
        self.free[p] = (sem, sem.n)
        self.consumed_idx += 1


def build_program(tap=None):
    nc = bass.Bass("TRN2", target_bir_lowering=False)

    d_x = nc.dram_tensor("x_tok", [TOK, D], F32, kind="ExternalInput").ap()
    d_wdkv = nc.dram_tensor("wdkv_h", [128, NF * 128], F32, kind="ExternalInput").ap()
    d_wq = nc.dram_tensor("wq_T", [D, D], F32, kind="ExternalInput").ap()
    d_wuk = nc.dram_tensor("wuk_h", [128, NH * 128], F32, kind="ExternalInput").ap()
    d_wuv = nc.dram_tensor("wuv_T", [128, NH * 128], F32, kind="ExternalInput").ap()
    d_wo = nc.dram_tensor("wo_T", [D, D], F32, kind="ExternalInput").ap()
    d_gw = nc.dram_tensor("gw_h", [128, NF * NE], F32, kind="ExternalInput").ap()
    d_sgv = nc.dram_tensor("sgv_h", [128, NF], F32, kind="ExternalInput").ap()
    d_sgu = nc.dram_tensor("sgu_c", [NI_S * 2, 128, NF * 128], F32, kind="ExternalInput").ap()
    d_sd = nc.dram_tensor("sd_T", [SI, D], F32, kind="ExternalInput").ap()
    d_egu = nc.dram_tensor("egu_c", [NE, NI_E * 2, 128, NF * 128], F32, kind="ExternalInput").ap()
    d_ed = nc.dram_tensor("ed_T", [NE, 2, NI_E, 128, 1024], F32, kind="ExternalInput").ap()
    d_alibi = nc.dram_tensor("alibi_c", [128, BAND * 128], F32, kind="ExternalInput").ap()
    d_ident = nc.dram_tensor("ident_c", [128, 128], F32, kind="ExternalInput").ap()
    d_ones = nc.dram_tensor("ones_c", [1, 128], F32, kind="ExternalInput").ap()
    d_L = nc.dram_tensor("Lh_c", [128, 128], F32, kind="ExternalInput").ap()
    d_U = nc.dram_tensor("Uh_c", [128, 128], F32, kind="ExternalInput").ap()
    d_iota = nc.dram_tensor("iota_c", [128, CAP], F32, kind="ExternalInput").ap()
    d_out = nc.dram_tensor("out_tok", [TOK, D], F32, kind="ExternalOutput").ap()
    d_xntm = nc.dram_tensor("xn_tm", [TB, D], BF16, kind="Internal").ap()

    with ExitStack() as ctx:
        eC = ctx.enter_context

        xnT = eC(nc.sbuf_tensor("xnT", [128, NF * TB], BF16))
        acc = eC(nc.sbuf_tensor("acc", [128, NF * TB], F32))
        ckvT = eC(nc.sbuf_tensor("ckvT", [128, S], BF16))
        ckvtm = eC(nc.sbuf_tensor("ckvtm", [128, NF * 128], BF16))
        obuf = eC(nc.sbuf_tensor("obuf", [128, HG * TB], BF16))
        qt4 = eC(nc.sbuf_tensor("qt4", [128, HG * TB], BF16))
        alibi = eC(nc.sbuf_tensor("alibi", [128, BAND * 128], BF16))
        wdkv = eC(nc.sbuf_tensor("wdkv", [128, NF * 128], BF16))
        wuk = eC(nc.sbuf_tensor("wuk", [128, NH * 128], BF16))
        wuvT = eC(nc.sbuf_tensor("wuvT", [128, NH * 128], BF16))
        gw = eC(nc.sbuf_tensor("gw", [128, NF * NE], BF16))
        sgv = eC(nc.sbuf_tensor("sgv", [128, NF], BF16))
        identb = eC(nc.sbuf_tensor("identb", [128, 128], BF16))
        onesb = eC(nc.sbuf_tensor("onesb", [1, 128], BF16))
        Lb = eC(nc.sbuf_tensor("Lb", [128, 128], BF16))
        Ub = eC(nc.sbuf_tensor("Ub", [128, 128], BF16))
        mskb_all = eC(nc.sbuf_tensor("mskb_all", [128, 128], BF16))
        comb_all = eC(nc.sbuf_tensor("comb_all", [128, 128], BF16))
        slotf = eC(nc.sbuf_tensor("slotf", [128, 128], F32))
        ubuf = eC(nc.sbuf_tensor("ubuf", [128, 2 * 4096], BF16))
        dbuf = eC(nc.sbuf_tensor("dbuf", [128, 3 * 512], BF16))
        wqb = eC(nc.sbuf_tensor("wqb", [128, 3 * 512], BF16))
        dstr2 = eC(nc.sbuf_tensor("dstr2", [128, 4096], BF16))
        R1f = eC(nc.sbuf_tensor("R1f", [128, 2048], F32))
        R2b = eC(nc.sbuf_tensor("R2b", [128, NI_E * TB], BF16))
        R3b = eC(nc.sbuf_tensor("R3b", [128, 2048], BF16))
        cbc = eC(nc.sbuf_tensor("cbc", [128, CAP], BF16))
        sbc = eC(nc.sbuf_tensor("sbc", [128, TB], BF16))
        rowb = eC(nc.sbuf_tensor("rowb", [1, TB], BF16))
        iotaC = eC(nc.sbuf_tensor("iotaC", [128, CAP], F32))
        stats = eC(nc.sbuf_tensor("stats", [128, 12], F32))
        sumr, ssq, negm, vep, lnv, rstd, nmax, sums, rsum, s4, l4, r4 = (
            stats[:, i:i + 1] for i in range(12))
        t8 = eC(nc.sbuf_tensor("t8", [128, 8], F32))
        Esb = eC(nc.sbuf_tensor("Esb", [128, NE], F32))
        msk = eC(nc.sbuf_tensor("msk", [128, NE], F32))
        cw = eC(nc.sbuf_tensor("cw", [128, NE], F32))
        comb = eC(nc.sbuf_tensor("comb", [128, NE], BF16))

        srw = rowb[:, 0:TB]
        crow = rowb[:, 0:CAP]

        psA = eC(nc.psum_tensor("psA", [128, 1024], F32))
        psB = eC(nc.psum_tensor("psB", [128, 1024], F32))
        psC = eC(nc.psum_tensor("psC", [128, 1024], F32))
        psD = eC(nc.psum_tensor("psD", [128, 1024], F32))
        PS = [psA, psB, psC, psD]

        sv = Sem(ctx, nc, "sv")
        sa = Sem(ctx, nc, "sa")
        st = Sem(ctx, nc, "st")
        ssm = Sem(ctx, nc, "ssm")
        sout = Sem(ctx, nc, "sout")
        s_idx = Sem(ctx, nc, "sidx")
        s_xn = Sem(ctx, nc, "sxn")
        s_acm = Sem(ctx, nc, "sacm")
        s_gx = Sem(ctx, nc, "sgx")
        s_sc = Sem(ctx, nc, "ssc")
        s_cv = Sem(ctx, nc, "scv")
        s_fi = Sem(ctx, nc, "sfi")

        plan = Plan()
        W = plan.wait
        st_wu = DStream(ctx, nc, "wu", plan, "g", n_bufs=2)
        st_wd = DStream(ctx, nc, "wd", plan, "g", n_bufs=3)
        st_wq = DStream(ctx, nc, "wq", plan, "g", n_bufs=3)
        st_do = DStream(ctx, nc, "do", plan, "g", n_bufs=2)
        st_x = DStream(ctx, nc, "sx", plan, "y")

        def mk_pump(stream, srcs, dst_fn):
            base = stream.idx
            state = {"next": 0}

            def pump():
                while (state["next"] < len(srcs)
                       and base + state["next"] <= stream.consumed_idx + stream.nb - 1):
                    j = state["next"]
                    stream.load(dst_fn(base + j), srcs[j])
                    state["next"] += 1
            return pump

        ps_free = {}
        silu_free = [None, None]

        def ps_wait(ps):
            fr = ps_free.get(ps.name, None)
            if fr is not None:
                W("t", fr[0], fr[1])

        def ps_release(ps, sem):
            ps_free[ps.name] = (sem, sem.n)

        def ubuf_slot(j):
            return ubuf[:, (j % 2) * 4096:((j % 2) + 1) * 4096].rearrange(
                "p (k c) -> p k c", c=2048)

        def dbuf_slot(j):
            return dbuf[:, (j % 3) * 512:((j % 3) + 1) * 512]

        def dq_slot(j):
            s_ = obuf[:, 0:4096] if j % 2 == 0 else dstr2[:, 0:4096]
            return s_.rearrange("p (k c) -> p k c", c=1024)

        def wqb_slot(j):
            return wqb[:, (j % 3) * 512:((j % 3) + 1) * 512]

        P_OFF, PT_OFF, P2_OFF = 0, 1152, 2304
        SPv = R2b[:, 3456:5760].bitcast(F32)
        QH_OFF = 5760
        HP_OFF, XG_OFF = 0, NI_E * CAP
        R2bF = R2b[:, :].bitcast(F32)
        qt4F = qt4[:, :].bitcast(F32)
        obufF = obuf[:, :].bitcast(F32)

        def T(fn, inc=None, k=1):
            if inc is None:
                plan.add("t", fn)
            else:
                plan.add("t", lambda E, f=fn, h=inc.h, kk=k: f(E).then_inc(h, kk))
                inc.n += k

        def V(fn, inc=None, k=1):
            if inc is None:
                plan.add("v", fn)
            else:
                plan.add("v", lambda E, f=fn, h=inc.h, kk=k: f(E).then_inc(h, kk))
                inc.n += k

        def A(fn, inc=None, k=1):
            if inc is None:
                plan.add("s", fn)
            else:
                plan.add("s", lambda E, f=fn, h=inc.h, kk=k: f(E).then_inc(h, kk))
                inc.n += k

        def G(fn, inc=None, k=16):
            if inc is None:
                plan.add("g", fn)
            else:
                plan.add("g", lambda E, f=fn, h=inc.h, kk=k: f(E).then_inc(h, kk))
                inc.n += k

        def Y(fn, inc=None, k=16):
            if inc is None:
                plan.add("y", fn)
            else:
                plan.add("y", lambda E, f=fn, h=inc.h, kk=k: f(E).then_inc(h, kk))
                inc.n += k

        def phase_gate():
            W("t", sv); W("t", sa)
            W("v", st); W("v", sa)
            W("s", st); W("s", sv)

        def vrt():
            V(lambda E: E.engine_nop(), inc=sv)
            W("s", sv)
            A(lambda E: E.nop(), inc=sa)
            W("v", sa)

        def tap_dump(name, ap):
            if tap is None:
                return
            dr = nc.dram_tensor("tap_" + name, list(ap.shape), ap.dtype,
                                kind="ExternalOutput").ap()
            W("y", sv); W("y", sa); W("y", st)
            Y(lambda E, d=dr, s_=ap: E.dma_start(out=d, in_=s_), inc=sout)

        def xn(g, t0, w):
            return xnT[:, g * TB + t0: g * TB + t0 + w]

        def accv(g, t0, w):
            return acc[:, g * TB + t0: g * TB + t0 + w]

        for dst, src in ((wdkv, d_wdkv), (wuk, d_wuk), (wuvT, d_wuv), (gw, d_gw),
                         (sgv, d_sgv), (identb, d_ident), (onesb, d_ones),
                         (alibi, d_alibi), (Lb, d_L), (Ub, d_U), (iotaC, d_iota)):
            plan.add("g", lambda E, d=dst, s_=src, h=ssm.h:
                     E.dma_start(out=d[:], in_=s_).then_inc(h, 16))
            ssm.n += 16
        for e_ in "tvs":
            W(e_, ssm)

        def xslot(gi):
            return R1f[:, 0:2048] if gi % 2 == 0 else R2bF[:, 0:2048]

        st_x.load(xslot(0), d_x[0:128, :])

        def p0(s_i, b_i):
            kb = s_i * NBLK_SEQ + b_i
            phase_gate()
            W("y", st)
            for tt in range(8):
                gt = kb * 8 + tt
                par, xsem, xval = st_x.take()
                xa = xslot(gt)
                if gt + 1 < TOK // 128:
                    st_x.load(xslot(gt + 1), d_x[(gt + 1) * 128:(gt + 2) * 128, :])
                W("v", xsem, xval)
                V(lambda E, x_=xa: E.reduce_sum(sumr, x_, axis=AX.X), inc=sv)
                W("s", sv)
                A(lambda E: E.nop(), inc=sa)
                W("v", sa)
                V(lambda E: E.tensor_scalar(negm, sumr, -1.0 / D, None, op0=ALU.mult),
                  inc=sv)
                W("s", sv)
                A(lambda E: E.nop(), inc=sa)
                W("v", sa)
                V(lambda E, x_=xa: E.tensor_scalar(R3b[:, 0:2048], x_, negm, None,
                                                   op0=ALU.add), inc=sv)
                st_x.consumed(par, sv)
                W("s", sv)
                A(lambda E: E.activation(R2b[:, 4096:6144], R3b[:, 0:2048], AF.Square,
                                         accum_out=ssq), inc=sa)
                W("v", sa)
                V(lambda E: E.tensor_scalar(vep, ssq, 1.0 / D, EPS,
                                            op0=ALU.mult, op1=ALU.add), inc=sv)
                W("s", sv)
                A(lambda E: E.activation(lnv, vep, AF.Ln), inc=sa)
                W("v", sa)
                V(lambda E: E.engine_nop(), inc=sv)
                W("s", sv)
                A(lambda E: E.activation(rstd, lnv, AF.Exp, scale=-0.5), inc=sa)
                W("v", sa)
                W("v", s_xn)
                V(lambda E: E.tensor_scalar(R2b[:, 6144:8192], R3b[:, 0:2048],
                                            rstd, None, op0=ALU.mult), inc=sv)
                W("y", sv)
                Y(lambda E, tt=tt: E.dma_start(out=d_xntm[tt * 128:(tt + 1) * 128, :],
                                               in_=R2b[:, 6144:8192]), inc=s_xn)
                W("t", sv)
                pdb = psD[:, :].bitcast(BF16)
                for g in range(NF):
                    T(lambda E, g=g, pdb=pdb: E.transpose(
                        pdb[:, g * 128:(g + 1) * 128],
                        R2b[:, 6144 + g * 128: 6144 + (g + 1) * 128], identb[:]),
                      inc=(st if g == NF - 1 else None))
                W("v", st)
                dst3 = xnT[:, :].rearrange("p (g t) -> p g t", t=TB)[:, :, tt * 128:(tt + 1) * 128]
                src3 = pdb.rearrange("p (g t) -> p g t", t=128)
                V(lambda E, d3=dst3, s3=src3: E.tensor_copy(d3, s3), inc=sv)

            W("t", sv)
            boff = b_i * TB
            ps_wait(psA)
            for th in range(2):
                for g in range(NF):
                    T(lambda E, th=th, g=g: E.matmul(
                        psA[:, th * 512:(th + 1) * 512],
                        wdkv[:, g * 128:(g + 1) * 128],
                        xn(g, th * 512, 512),
                        start=(g == 0), stop=(g == NF - 1)),
                      inc=(st if (th == 1 and g == NF - 1) else None))
            W("s", st)
            A(lambda E, bo=boff: E.copy(ckvT[:, bo:bo + TB], psA[:, :]), inc=sa)
            ps_release(psA, sa)
            W("t", sa)
            pdb = psD[:, :].bitcast(BF16)
            for jc in range(8):
                T(lambda E, jc=jc, bo=boff, pdb=pdb: E.transpose(
                    pdb[:, jc * 128:(jc + 1) * 128],
                    ckvT[:, bo + jc * 128: bo + (jc + 1) * 128], identb[:]),
                  inc=(st if jc == 7 else None))
            W("v", st)
            dst3 = ckvtm[:, (b_i * 8) * 128:(b_i * 8 + 8) * 128]
            V(lambda E, d_=dst3, pdb=pdb: E.tensor_copy(d_, pdb[:, 0:1024]), inc=sv)

            psI = psB[:, 512:528]
            for tt in range(8):
                W("t", sv)
                for g in range(NF):
                    T(lambda E, tt=tt, g=g: E.matmul(
                        psB[:, 0:NE],
                        xn(g, tt * 128, 128),
                        gw[:, g * NE:(g + 1) * NE],
                        start=(g == 0), stop=(g == NF - 1)),
                      inc=(st if g == NF - 1 else None))
                W("v", st)
                V(lambda E: E.reduce_max(nmax, psB[:, 0:NE], axis=AX.X, negate=True), inc=sv)
                W("s", sv)
                A(lambda E: E.activation(Esb[:], psB[:, 0:NE], AF.Exp, bias=nmax), inc=sa)
                W("v", sa)
                V(lambda E: E.max(out=t8[:], in_=Esb[:]), inc=sv)
                W("s", sv)
                A(lambda E: E.nop(), inc=sa)
                W("v", sa)
                V(lambda E: E.tensor_scalar(msk[:], Esb[:], t8[:, 3:4], None, op0=ALU.is_ge),
                  inc=sv)
                W("s", sv)
                A(lambda E: E.nop(), inc=sa)
                W("v", sa)
                V(lambda E: E.tensor_mul(cw[:], Esb[:], msk[:]))
                V(lambda E, tt=tt: E.tensor_copy(mskb_all[:, tt * NE:(tt + 1) * NE], msk[:]),
                  inc=sv)
                W("s", sv)
                A(lambda E: E.nop(), inc=sa)
                W("v", sa)
                V(lambda E: E.tensor_scalar(msk[:], msk[:], -4096.0, 4096.0,
                                            op0=ALU.mult, op1=ALU.add), inc=sv)
                W("s", sv)
                A(lambda E: E.nop(), inc=sa)
                W("v", sa)
                V(lambda E: E.reduce_sum(s4, cw[:], axis=AX.X), inc=sv)
                W("s", sv)
                A(lambda E: E.activation(l4, s4, AF.Ln), inc=sa)
                W("v", sa)
                V(lambda E: E.engine_nop(), inc=sv)
                W("s", sv)
                A(lambda E: E.activation(r4, l4, AF.Exp, scale=-1.0), inc=sa)
                W("v", sa)
                V(lambda E: E.tensor_scalar(comb[:], cw[:], r4, None, op0=ALU.mult), inc=sv)
                W("t", sv)
                T(lambda E, tt=tt: E.matmul(psI, Lb[:], mskb_all[:, tt * NE:(tt + 1) * NE],
                                            start=(tt == 0), stop=False), inc=st)
                W("v", st)
                V(lambda E, tt=tt: E.scalar_tensor_tensor(
                    slotf[:, tt * NE:(tt + 1) * NE], psI, 1.0, msk[:],
                    op0=ALU.mult, op1=ALU.add))
                V(lambda E, tt=tt: E.tensor_copy(comb_all[:, tt * NE:(tt + 1) * NE],
                                                 comb[:]), inc=sv)
                W("t", sv)
                T(lambda E, tt=tt: E.matmul(psI, Ub[:], mskb_all[:, tt * NE:(tt + 1) * NE],
                                            start=False, stop=(tt == 7)), inc=st)

            W("t", sv)
            for th in range(2):
                for g in range(NF):
                    T(lambda E, th=th, g=g: E.matmul(
                        psB[0:1, th * 512:(th + 1) * 512],
                        sgv[:, g:g + 1],
                        xn(g, th * 512, 512),
                        start=(g == 0), stop=(g == NF - 1)),
                      inc=(st if (th == 1 and g == NF - 1) else None))
            W("s", st)
            A(lambda E: E.activation(srw, psB[0:1, :], AF.Sigmoid), inc=sa)
            ps_wait(psC)
            W("t", sa)
            for th in range(2):
                T(lambda E, th=th: E.matmul(psC[:, th * 512:(th + 1) * 512], onesb[:],
                                            srw[:, th * 512:(th + 1) * 512],
                                            start=True, stop=True),
                  inc=(st if th == 1 else None))
            W("v", st)
            V(lambda E: E.tensor_copy(sbc[:], psC[:, :]), inc=sv)
            ps_release(psC, sv)
            ps_release(psB, sv)
            ps_release(psD, sv)

        def attn(s_i, b_i, hg):
            phase_gate()
            if hg == 0:
                W("s", s_acm)
                W("v", s_acm)
            pump_q = mk_pump(st_wq,
                             [d_wq[f * 128:(f + 1) * 128, hg * 512:(hg + 1) * 512]
                              for f in range(NF)],
                             wqb_slot)
            pump_q()
            for f in range(NF):
                par, qsem, qval = st_wq.take()
                W("t", qsem, qval)
                if f == 0:
                    for p_ in PS:
                        ps_wait(p_)
                for hh in range(HG):
                    for th in range(2):
                        T(lambda E, f=f, hh=hh, th=th, par=par: E.matmul(
                            PS[hh][:, th * 512:(th + 1) * 512],
                            wqb[:, par * 512 + hh * 128: par * 512 + (hh + 1) * 128],
                            xn(f, th * 512, 512),
                            start=(f == 0), stop=(f == NF - 1)),
                          inc=(st if (hh == HG - 1 and th == 1) else None))
                st_wq.consumed(par, st)
                pump_q()
            W("s", st)
            for hh in range(HG):
                A(lambda E, hh=hh: E.copy(R2b[:, QH_OFF + hh * TB: QH_OFF + (hh + 1) * TB],
                                          PS[hh][:, :]), inc=(sa if hh == HG - 1 else None))
            W("t", sa)
            for hh in range(HG):
                h = hg * HG + hh
                for th in range(2):
                    T(lambda E, hh=hh, h=h, th=th: E.matmul(
                        PS[hh][:, th * 512:(th + 1) * 512],
                        wuk[:, h * 128:(h + 1) * 128],
                        R2b[:, QH_OFF + hh * TB + th * 512: QH_OFF + hh * TB + (th + 1) * 512],
                        start=True, stop=True),
                      inc=(st if (hh == HG - 1 and th == 1) else None))
            W("s", st)
            for hh in range(HG):
                A(lambda E, hh=hh: E.activation(qt4[:, hh * TB:(hh + 1) * TB], PS[hh][:, :],
                                                AF.Identity, scale=SCALE),
                  inc=(sa if hh == HG - 1 else None))
            W("t", sa); W("v", sa)

            pBb = psB[:, :].bitcast(BF16)
            for qt in range(8):
                Qg = b_i * 8 + qt
                js = max(0, Qg - (BAND - 1))
                j0b = js * 128
                wband = (Qg + 1) * 128 - j0b
                for hh in range(HG):
                    h = hg * HG + hh
                    nchunks = (wband + 511) // 512
                    for jc5 in range(nchunks):
                        jo = jc5 * 512
                        w_ = min(512, wband - jo)
                        T(lambda E, hh=hh, qt=qt, jo=jo, w_=w_, p_=(jc5 % 2), j0b=j0b: E.matmul(
                            psA[:, p_ * 512: p_ * 512 + w_],
                            qt4[:, hh * TB + qt * 128: hh * TB + (qt + 1) * 128],
                            ckvT[:, j0b + jo: j0b + jo + w_],
                            start=True, stop=True), inc=st)
                        W("v", st)
                        V(lambda E, jo=jo, w_=w_, p_=(jc5 % 2), wb=wband: E.tensor_tensor(
                            SPv[:, jo: jo + w_],
                            psA[:, p_ * 512: p_ * 512 + w_],
                            alibi[:, BAND * 128 - wb + jo: BAND * 128 - wb + jo + w_],
                            op=ALU.add), inc=sv)
                        W("t", sv)
                    V(lambda E, wb=wband: E.reduce_max(nmax, SPv[:, 0:wb],
                                                       axis=AX.X, negate=True), inc=sv)
                    W("s", sv)
                    A(lambda E, wb=wband: E.activation(R2b[:, P_OFF:P_OFF + wb],
                                                       SPv[:, 0:wb], AF.Exp,
                                                       bias=nmax, accum_out=sums), inc=sa)
                    W("v", sa)
                    V(lambda E: E.reciprocal(rsum, sums), inc=sv)
                    W("s", sv)
                    A(lambda E: E.nop(), inc=sa)
                    W("v", sa)
                    V(lambda E, wb=wband: E.tensor_scalar(R2b[:, P2_OFF:P2_OFF + wb],
                                                          R2b[:, P_OFF:P_OFF + wb],
                                                          rsum, None, op0=ALU.mult),
                      inc=sv)
                    W("t", sv)
                    nj = wband // 128
                    for jc in range(nj):
                        T(lambda E, jc=jc: E.transpose(
                            pBb[:, jc * 128:(jc + 1) * 128],
                            R2b[:, P2_OFF + jc * 128: P2_OFF + (jc + 1) * 128],
                            identb[:]),
                          inc=(st if jc == nj - 1 else None))
                    W("s", st)
                    A(lambda E, wb=wband: E.copy(R2b[:, PT_OFF:PT_OFF + wb], pBb[:, 0:wb]),
                      inc=sa)
                    W("t", sa)
                    for jc in range(nj):
                        T(lambda E, jc=jc, js=js, nj=nj: E.matmul(
                            psC[:, 0:128],
                            ckvtm[:, (js + jc) * 128:(js + jc + 1) * 128],
                            R2b[:, PT_OFF + jc * 128: PT_OFF + (jc + 1) * 128],
                            start=(jc == 0), stop=(jc == nj - 1)),
                          inc=(st if jc == nj - 1 else None))
                    W("v", st)
                    V(lambda E: E.tensor_copy(R3b[:, 0:128], psC[:, 0:128]), inc=sv)
                    W("t", sv)
                    T(lambda E, h=h: E.matmul(psC[:, 512:640],
                                              wuvT[:, h * 128:(h + 1) * 128],
                                              R3b[:, 0:128], start=True, stop=True), inc=st)
                    W("s", st)
                    A(lambda E, hh=hh, qt=qt: E.copy(
                        obuf[:, hh * TB + qt * 128: hh * TB + (qt + 1) * 128],
                        psC[:, 512:640]), inc=sa)
            W("v", sa)
            V(lambda E: E.engine_nop(), inc=sv)
            for p_ in PS:
                ps_release(p_, sv)

        def woadd(s_i, b_i, hg, init):
            phase_gate()
            pump_wo = mk_pump(st_wd,
                              [d_wo[(hg * HG + hh) * 128:(hg * HG + hh + 1) * 128,
                                    gq * 512:(gq + 1) * 512]
                               for gq in range(4) for hh in range(HG)],
                              dbuf_slot)
            pump_wo()
            for gq in range(4):
                for hh in range(HG):
                    par, dsem, dval = st_wd.take()
                    W("t", dsem, dval)
                    for gl in range(4):
                        if hh == 0:
                            ps_wait(PS[gl])
                        for th in range(2):
                            T(lambda E, par=par, gl=gl, th=th, hh=hh: E.matmul(
                                PS[gl][:, th * 512:(th + 1) * 512],
                                dbuf[:, par * 512 + gl * 128: par * 512 + (gl + 1) * 128],
                                obuf[:, hh * TB + th * 512: hh * TB + (th + 1) * 512],
                                start=(hh == 0), stop=(hh == HG - 1)),
                              inc=(st if (gl == 3 and th == 1) else None))
                    st_wd.consumed(par, st)
                    pump_wo()
                W("v", st)
                for gl in range(4):
                    g = gq * 4 + gl
                    if init:
                        V(lambda E, g=g, gl=gl: E.tensor_copy(accv(g, 0, TB), PS[gl][:, :]),
                          inc=sv)
                    else:
                        V(lambda E, g=g, gl=gl: E.tensor_tensor(accv(g, 0, TB), PS[gl][:, :],
                                                                accv(g, 0, TB), op=ALU.add),
                          inc=sv)
                    ps_release(PS[gl], sv)

        def ffn_up_pairs_dense(npairs, bcast_sb, srcs):
            pump_u = mk_pump(st_wu, srcs, ubuf_slot)
            pump_u()
            for k in range(npairs):
                psg = PS[(2 * k) % 4]
                psu = PS[(2 * k + 1) % 4]
                sgo = (k % 2) * TB
                par, usem, uval = st_wu.take()
                W("t", usem, uval)
                ps_wait(psg)
                for g in range(NF):
                    for th in range(2):
                        T(lambda E, par=par, g=g, th=th, ps_=psg: E.matmul(
                            ps_[:, th * 512:(th + 1) * 512],
                            ubuf[:, par * 4096 + g * 128: par * 4096 + (g + 1) * 128],
                            xn(g, th * 512, 512),
                            start=(g == 0), stop=(g == NF - 1)),
                          inc=(st if (g == NF - 1 and th == 1) else None))
                g_done = st.n
                ps_wait(psu)
                for g in range(NF):
                    for th in range(2):
                        T(lambda E, par=par, g=g, th=th, ps_=psu: E.matmul(
                            ps_[:, th * 512:(th + 1) * 512],
                            ubuf[:, par * 4096 + 2048 + g * 128: par * 4096 + 2048 + (g + 1) * 128],
                            xn(g, th * 512, 512),
                            start=(g == 0), stop=(g == NF - 1)),
                          inc=(st if (g == NF - 1 and th == 1) else None))
                u_done = st.n
                st_wu.consumed(par, st)
                pump_u()
                W("s", st, g_done)
                if silu_free[k % 2] is not None:
                    W("s", silu_free[k % 2][0], silu_free[k % 2][1])
                A(lambda E, ps_=psg, sgo=sgo: E.activation(R3b[:, sgo:sgo + TB],
                                                           ps_[:, :], AF.Silu), inc=sa)
                ps_release(psg, sa)
                W("v", st, u_done); W("v", sa)
                V(lambda E, ps_=psu, bc=bcast_sb, k=k: E.tensor_tensor(
                    R2b[:, k * TB:(k + 1) * TB], ps_[:, :], bc[:], op=ALU.mult))
                V(lambda E, k=k, sgo=sgo: E.tensor_tensor(
                    R2b[:, k * TB:(k + 1) * TB], R2b[:, k * TB:(k + 1) * TB],
                    R3b[:, sgo:sgo + TB], op=ALU.mult), inc=sv)
                ps_release(psu, sv)
                silu_free[k % 2] = (sv, sv.n)

        def ffn_down_dense(nchunks, slab_fn, init):
            pump_d = mk_pump(st_wd,
                             [slab_fn(i, gq) for gq in range(4) for i in range(nchunks)],
                             dbuf_slot)
            pump_d()
            for gq in range(4):
                for i in range(nchunks):
                    par, dsem, dval = st_wd.take()
                    W("t", dsem, dval)
                    for gl in range(4):
                        if i == 0:
                            ps_wait(PS[gl])
                        for th in range(2):
                            T(lambda E, par=par, gl=gl, th=th, i=i: E.matmul(
                                PS[gl][:, th * 512:(th + 1) * 512],
                                dbuf[:, par * 512 + gl * 128: par * 512 + (gl + 1) * 128],
                                R2b[:, i * TB + th * 512: i * TB + (th + 1) * 512],
                                start=(i == 0), stop=(i == nchunks - 1)),
                              inc=(st if (gl == 3 and th == 1) else None))
                    st_wd.consumed(par, st)
                    pump_d()
                W("v", st)
                for gl in range(4):
                    g = gq * 4 + gl
                    if init:
                        V(lambda E, g=g, gl=gl: E.tensor_copy(accv(g, 0, TB), PS[gl][:, :]),
                          inc=sv)
                    else:
                        V(lambda E, g=g, gl=gl: E.tensor_tensor(accv(g, 0, TB), PS[gl][:, :],
                                                                accv(g, 0, TB), op=ALU.add),
                          inc=sv)
                    ps_release(PS[gl], sv)

        def shared(s_i, b_i):
            phase_gate()
            for gr in range(4):
                srcs = [d_sgu[2 * (gr * 11 + k):2 * (gr * 11 + k) + 2, :, :]
                        .rearrange("k p c -> p k c") for k in range(11)]
                ffn_up_pairs_dense(11, sbc, srcs)
                ffn_down_dense(11, lambda i, gq, gr=gr:
                               d_sd[(gr * 11 + i) * 128:(gr * 11 + i + 1) * 128,
                                    gq * 512:(gq + 1) * 512],
                               init=(gr == 0))

        def expert_gather(e, b_i):
            W("v", st)
            for tt in range(8):
                V(lambda E, tt=tt, e=e: E.tensor_scalar(
                    qt4[:, tt * CAP:(tt + 1) * CAP], iotaC[:],
                    slotf[:, tt * NE + e: tt * NE + e + 1], None,
                    op0=ALU.is_equal), inc=(sv if tt == 7 else None))
            gdst = [psA[:, 0:CAP], psA[:, 512:512 + CAP],
                    psB[:, 0:CAP], psB[:, 512:512 + CAP],
                    psC[:, 0:CAP], psC[:, 512:512 + CAP],
                    psD[:, 0:CAP], psD[:, 512:512 + CAP]]
            W("t", sv)
            W("t", s_gx)
            for half in range(2):
                for p_ in PS:
                    ps_wait(p_)
                for tt in range(8):
                    for fc8 in range(8):
                        fc = half * 8 + fc8
                        T(lambda E, tt=tt, fc=fc, d_=gdst[fc8]: E.matmul(
                            d_, xnT[:, tt * 2048 + fc * 128: tt * 2048 + (fc + 1) * 128],
                            qt4[:, tt * CAP:(tt + 1) * CAP],
                            start=(tt == 0), stop=(tt == 7)),
                          inc=(st if (tt == 7 and fc8 == 7) else None))
                W("s", st)
                for fc8 in range(8):
                    fc = half * 8 + fc8
                    A(lambda E, fc=fc, s_=gdst[fc8]: E.copy(
                        R2b[:, XG_OFF + fc * CAP: XG_OFF + (fc + 1) * CAP], s_),
                      inc=(sa if fc8 == 7 else None))
                for p_ in PS:
                    ps_release(p_, sa)
            ps_wait(psA)
            for tt in range(8):
                T(lambda E, tt=tt, e=e: E.matmul(
                    psA[0:1, 0:CAP],
                    comb_all[:, tt * NE + e: tt * NE + e + 1],
                    qt4[:, tt * CAP:(tt + 1) * CAP],
                    start=(tt == 0), stop=(tt == 7)),
                  inc=(st if tt == 7 else None))
            W("v", st)
            V(lambda E: E.tensor_copy(crow, psA[0:1, 0:CAP]), inc=sv)
            ps_release(psA, sv)
            W("t", sv)
            ps_wait(psD)
            T(lambda E: E.matmul(psD[:, 0:CAP], onesb[:], crow, start=True, stop=True),
              inc=st)
            W("v", st)
            V(lambda E: E.tensor_copy(cbc[:, 0:CAP], psD[:, 0:CAP]), inc=sv)
            ps_release(psD, sv)

        def expert_up(e):
            srcs = [d_egu[e, 2 * k:2 * k + 2, :, :].rearrange("k p c -> p k c")
                    for k in range(11)]
            pump_u = mk_pump(st_wu, srcs, ubuf_slot)
            pump_u()
            W("t", sa)
            for k in range(11):
                psg = [psA, psC][k % 2]
                psu = [psB, psD][k % 2]
                sgo = (k % 2) * TB
                par, usem, uval = st_wu.take()
                W("t", usem, uval)
                ps_wait(psg)
                for c in range(NF):
                    T(lambda E, par=par, c=c, ps_=psg: E.matmul(
                        ps_[:, 0:CAP],
                        ubuf[:, par * 4096 + c * 128: par * 4096 + (c + 1) * 128],
                        R2b[:, XG_OFF + c * CAP: XG_OFF + (c + 1) * CAP],
                        start=(c == 0), stop=(c == NF - 1)),
                      inc=(st if c == NF - 1 else None))
                g_done = st.n
                ps_wait(psu)
                for c in range(NF):
                    T(lambda E, par=par, c=c, ps_=psu: E.matmul(
                        ps_[:, 0:CAP],
                        ubuf[:, par * 4096 + 2048 + c * 128: par * 4096 + 2048 + (c + 1) * 128],
                        R2b[:, XG_OFF + c * CAP: XG_OFF + (c + 1) * CAP],
                        start=(c == 0), stop=(c == NF - 1)),
                      inc=(st if c == NF - 1 else None))
                u_done = st.n
                st_wu.consumed(par, st)
                pump_u()
                W("s", st, g_done)
                if silu_free[k % 2] is not None:
                    W("s", silu_free[k % 2][0], silu_free[k % 2][1])
                A(lambda E, ps_=psg, sgo=sgo: E.activation(R3b[:, sgo:sgo + CAP],
                                                           ps_[:, 0:CAP], AF.Silu), inc=sa)
                ps_release(psg, sa)
                W("v", st, u_done); W("v", sa)
                V(lambda E, ps_=psu, k=k: E.tensor_tensor(
                    R2b[:, HP_OFF + k * CAP: HP_OFF + (k + 1) * CAP],
                    ps_[:, 0:CAP], cbc[:, 0:CAP], op=ALU.mult))
                V(lambda E, k=k, sgo=sgo: E.tensor_tensor(
                    R2b[:, HP_OFF + k * CAP: HP_OFF + (k + 1) * CAP],
                    R2b[:, HP_OFF + k * CAP: HP_OFF + (k + 1) * CAP],
                    R3b[:, sgo:sgo + CAP], op=ALU.mult), inc=sv)
                ps_release(psu, sv)
                silu_free[k % 2] = (sv, sv.n)

        def expert_down_scatter(e, b_i):
            quads = ((0, 4), (4, 4), (8, 3))
            srcs = [d_ed[e, gh, qa:qa + ln, :, :].rearrange("k p c -> p k c")
                    for gh in range(2) for qa, ln in quads]
            pump_d = mk_pump(st_do, srcs,
                             lambda j: dq_slot(j)[:, 0:quads[j % 3][1], :])
            pump_d()
            GTb = qt4[:, :]
            for gh in range(2):
                for qa, ln in quads:
                    par, dsem, dval = st_do.take()
                    slab = obuf[:, 0:4096] if par == 0 else dstr2[:, 0:4096]
                    W("t", dsem, dval)
                    for ii in range(ln):
                        i = qa + ii
                        for sc in range(NGC):
                            if i == 0:
                                ps_wait(PS[sc])
                            for ql in range(2):
                                T(lambda E, slab=slab, ii=ii, sc=sc, ql=ql, i=i: E.matmul(
                                    PS[sc][:, ql * 512:(ql + 1) * 512],
                                    R2b[:, HP_OFF + i * CAP + sc * 128: HP_OFF + i * CAP + (sc + 1) * 128],
                                    slab[:, ii * 1024 + ql * 512: ii * 1024 + (ql + 1) * 512],
                                    start=(i == 0), stop=(i == 10)),
                                  inc=(st if (sc == NGC - 1 and ql == 1) else None))
                    st_do.consumed(par, st)
                    pump_d()
                W("s", st)
                for sc in range(NGC):
                    A(lambda E, sc=sc, gh=gh: E.copy(
                        R2b[:, XG_OFF + sc * 2048 + gh * 1024: XG_OFF + sc * 2048 + (gh + 1) * 1024],
                        PS[sc][:, :]), inc=sa)
                    ps_release(PS[sc], sa)
                if gh == 0:
                    ps_wait(psD)
                    W("v", sa)
                    evac_done = [0, 0]
                    for tt in range(8):
                        rco = (tt % 2) * CAP
                        bko = (tt % 2) * 1024
                        pDb = psD[:, :].bitcast(BF16)
                        W("v", st)
                        V(lambda E, tt=tt, e=e, rco=rco: E.tensor_scalar(
                            R3b[:, rco:rco + CAP], iotaC[:],
                            slotf[:, tt * NE + e: tt * NE + e + 1], None,
                            op0=ALU.is_equal), inc=sv)
                        W("t", sv)
                        if evac_done[tt % 2]:
                            W("t", sa, evac_done[tt % 2])
                        for sc in range(NGC):
                            T(lambda E, sc=sc, rco=rco, bko=bko, pDb=pDb: E.transpose(
                                pDb[:, bko + sc * 128: bko + (sc + 1) * 128],
                                R3b[:, rco + sc * 128: rco + (sc + 1) * 128],
                                identb[:]),
                              inc=(st if sc == NGC - 1 else None))
                        W("s", st)
                        dst3 = GTb[:, 0:3 * TB].rearrange("p (sc t) -> p sc t", t=TB)[:, :, tt * 128:(tt + 1) * 128]
                        src3 = pDb[:, bko:bko + NGC * 128].rearrange("p (sc t) -> p sc t", t=128)
                        A(lambda E, d3=dst3, s3=src3: E.copy(d3, s3), inc=sa)
                        evac_done[tt % 2] = sa.n
                    ps_release(psD, sa)
            W("t", sa)
            for gq in range(4):
                for gl in range(4):
                    g = gq * 4 + gl
                    ps_wait(PS[gl])
                    for sc in range(NGC):
                        for th in range(2):
                            T(lambda E, g=g, sc=sc, th=th, gl=gl: E.matmul(
                                PS[gl][:, th * 512:(th + 1) * 512],
                                R2b[:, XG_OFF + sc * 2048 + g * 128: XG_OFF + sc * 2048 + (g + 1) * 128],
                                GTb[:, sc * TB + th * 512: sc * TB + (th + 1) * 512],
                                start=(sc == 0), stop=(sc == NGC - 1)),
                              inc=(st if (sc == NGC - 1 and th == 1) else None))
                W("v", st)
                for gl in range(4):
                    g = gq * 4 + gl
                    V(lambda E, g=g, gl=gl: E.tensor_tensor(accv(g, 0, TB), PS[gl][:, :],
                                                            accv(g, 0, TB), op=ALU.add),
                      inc=sv)
                    ps_release(PS[gl], sv)

        def load_tiles():
            phase_gate()
            st_do.free[0] = (st, st.n)
            st_do.free[1] = (st, st.n)
            W("y", st)
            W("y", s_xn)
            for tt in range(8):
                Y(lambda E, tt=tt: E.dma_start(
                    out=xnT[:, tt * 2048:(tt + 1) * 2048],
                    in_=d_xntm[tt * 128:(tt + 1) * 128, :]), inc=s_gx)

        def experts(s_i, b_i):
            load_tiles()
            for e in range(NE):
                expert_gather(e, b_i)
                expert_up(e)
                expert_down_scatter(e, b_i)

        def final(s_i, b_i):
            kb = s_i * NBLK_SEQ + b_i
            phase_gate()
            W("y", st); W("y", sv); W("y", sa)
            W("y", s_sc)
            pDb = psD[:, :].bitcast(BF16)
            fin_free = [None, None]
            for tt in range(8):
                gt = kb * 8 + tt
                par = tt % 2
                am = qt4F[:, 0:2048] if par == 0 else obufF[:, 0:2048]
                if fin_free[par] is not None:
                    W("y", fin_free[par][0], fin_free[par][1])
                Y(lambda E, gt=gt, am=am: E.dma_start(out=am, in_=d_x[gt * 128:(gt + 1) * 128, :]),
                  inc=s_fi)
                ld_done = s_fi.n
                src3 = acc[:, :].rearrange("p (g t) -> p g t", t=TB)[:, :, tt * 128:(tt + 1) * 128]
                W("v", st)
                V(lambda E, s3=src3: E.tensor_copy(R3b[:, 0:2048].rearrange(
                    "p (g t) -> p g t", t=128), s3), inc=sv)
                W("t", sv)
                ps_wait(psD)
                for g in range(NF):
                    T(lambda E, g=g, pDb=pDb: E.transpose(
                        pDb[:, g * 128:(g + 1) * 128],
                        R3b[:, g * 128:(g + 1) * 128], identb[:]),
                      inc=(st if g == NF - 1 else None))
                W("v", st)
                W("v", s_fi, ld_done)
                V(lambda E, am=am, pDb=pDb: E.tensor_tensor(am, pDb, am, op=ALU.add), inc=sv)
                ps_release(psD, sv)
                W("y", sv)
                Y(lambda E, gt=gt, am=am: E.dma_start(out=d_out[gt * 128:(gt + 1) * 128, :],
                                                      in_=am), inc=sout)
                fin_free[par] = (sout, sout.n)

        def block(s_i, b_i):
            p0(s_i, b_i)
            if tap == "p0":
                tap_dump("xnT", xnT[:, :])
                tap_dump("ckvT", ckvT[:, :])
                tap_dump("sbc", sbc[:, :])
                tap_dump("slotf", slotf[:, :])
                return True
            attn(s_i, b_i, 0)
            if tap == "attn":
                tap_dump("obuf", obuf[:, :])
                tap_dump("qt4", qt4[:, :])
                tap_dump("ckvT", ckvT[:, :])
                return True
            if tap == "expg":
                shared(s_i, b_i)
                load_tiles()
                expert_gather(0, b_i)
                tap_dump("xgT", R2b[:, XG_OFF:XG_OFF + NF * CAP])
                tap_dump("cbc", cbc[:, :])
                tap_dump("slotf", slotf[:, :])
                tap_dump("mskb", mskb_all[:, :])
                tap_dump("comba", comb_all[:, :])
                return True
            if tap == "exp2":
                shared(s_i, b_i)
                load_tiles()
                expert_gather(0, b_i)
                expert_up(0)
                tap_dump("hp", R2b[:, HP_OFF:HP_OFF + 11 * CAP])
                tap_dump("mskb", mskb_all[:, :])
                tap_dump("comba", comb_all[:, :])
                return True
            if tap == "exp3":
                shared(s_i, b_i)
                load_tiles()
                expert_gather(0, b_i)
                expert_up(0)
                expert_down_scatter(0, b_i)
                tap_dump("accp", acc[:, :])
                tap_dump("mskb", mskb_all[:, :])
                tap_dump("comba", comb_all[:, :])
                return True
            if tap == "exp1":
                shared(s_i, b_i)
                experts(s_i, b_i)
                tap_dump("accp", acc[:, :])
                tap_dump("slotf", slotf[:, :])
                tap_dump("xgT", R2b[:, XG_OFF:XG_OFF + NF * CAP])
                tap_dump("cbc", cbc[:, :])
                return True
            shared(s_i, b_i)
            woadd(s_i, b_i, 0, init=False)
            if tap == "sha":
                tap_dump("accp", acc[:, :])
                return True
            for hg in range(1, 4):
                attn(s_i, b_i, hg)
                woadd(s_i, b_i, hg, init=False)
            experts(s_i, b_i)
            final(s_i, b_i)
            return False

        stop = False
        for s_i in range(SEQ_PC):
            for b_i in range(NBLK_SEQ):
                stop = block(s_i, b_i)
                if stop or tap == "blk1":
                    stop = True
                    break
            if stop:
                break

        W("y", sout)

        blk = eC(nc.Block())
        blk.tensor(lambda E: [f(E) for f in plan.q["t"]] and None)
        blk.vector(lambda E: [f(E) for f in plan.q["v"]] and None)
        blk.scalar(lambda E: [f(E) for f in plan.q["s"]] and None)
        blk.gpsimd(lambda E: [f(E) for f in plan.q["g"]] and None)
        blk.sync(lambda E: [f(E) for f in plan.q["y"]] and None)

    return nc


def _stage_weights(w):
    f32 = lambda a: np.ascontiguousarray(np.asarray(a), dtype=np.float32)
    out = {}
    wdkvT = f32(w["L0_attn_w_dkv"]).T
    out["wdkv_h"] = np.ascontiguousarray(
        wdkvT.reshape(NF, 128, 128).transpose(1, 0, 2).reshape(128, NF * 128))
    out["wq_T"] = f32(w["L0_attn_w_q"]).T.copy()
    out["wuk_h"] = np.ascontiguousarray(
        f32(w["L0_attn_w_uk"]).reshape(NH, 128, 128).transpose(1, 0, 2).reshape(128, NH * 128))
    out["wuv_T"] = f32(w["L0_attn_w_uv"]).T.copy()
    out["wo_T"] = f32(w["L0_attn_w_o"]).T.copy()
    out["gw_h"] = np.ascontiguousarray(
        f32(w["L0_ffn_gate_w"]).T.reshape(NF, 128, NE).transpose(1, 0, 2).reshape(128, NF * NE))
    out["sgv_h"] = np.ascontiguousarray(
        f32(w["L0_ffn_shared_gate_vec"]).reshape(NF, 128).T)
    sguT = f32(w["L0_ffn_shared_gate_up"]).T
    sgu_c = sguT.reshape(NF, 128, NI_S * 2, 128).transpose(2, 1, 0, 3).reshape(NI_S * 2, 128, NF * 128)
    s_ord = []
    for j in range(NI_S):
        s_ord += [j, NI_S + j]
    out["sgu_c"] = np.ascontiguousarray(sgu_c[s_ord])
    out["sd_T"] = f32(w["L0_ffn_shared_down"]).T.copy()
    eguT = f32(w["L0_ffn_expert_gate_up"]).transpose(0, 2, 1)
    egu_c = (eguT.reshape(NE, NF, 128, NI_E * 2, 128).transpose(0, 3, 2, 1, 4)
             .reshape(NE, NI_E * 2, 128, NF * 128))
    e_ord = []
    for j in range(NI_E):
        e_ord += [j, NI_E + j]
    out["egu_c"] = np.ascontiguousarray(egu_c[:, e_ord])
    edT = f32(w["L0_ffn_expert_down"]).transpose(0, 2, 1)
    out["ed_T"] = np.ascontiguousarray(
        edT.reshape(NE, NI_E, 128, 2, 1024).transpose(0, 3, 1, 2, 4))
    p = np.arange(128)[:, None]
    f = np.arange(128)[None, :]
    al = np.empty((128, BAND * 128), np.float32)
    for m in range(BAND):
        k = BAND - 1 - m
        n = 128 * k + p - f
        al[:, m * 128:(m + 1) * 128] = np.where(n >= 0, -0.0625 * n, NEG_BIG)
    out["alibi_c"] = al
    out["ident_c"] = np.eye(128, dtype=np.float32)
    out["ones_c"] = np.ones((1, 128), np.float32)
    out["Lh_c"] = np.triu(np.ones((128, 128), np.float32), 1)
    out["iota_c"] = np.tile(np.arange(CAP, dtype=np.float32)[None, :], (128, 1))
    out["Uh_c"] = np.tril(np.ones((128, 128), np.float32))
    return out


def kernel(**inputs):
    x = np.asarray(inputs["L0_attn_hidden_states"], np.float32)
    shared_w = _stage_weights(inputs)
    nc = build_program()
    in_maps = []
    for c in range(NCORE):
        xc = np.ascontiguousarray(x[SEQ_PC * c: SEQ_PC * (c + 1)].reshape(TOK, D))
        m = dict(shared_w)
        m["x_tok"] = xc
        in_maps.append(m)
    res = run_bass_kernel_spmd(nc, in_maps, core_ids=list(range(NCORE)))
    outs = []
    for c in range(NCORE):
        o = np.asarray(res.results[c]["out_tok"], np.float32)
        outs.append(o.reshape(SEQ_PC, S, D))
    return np.concatenate(outs, axis=0)
